# version 26
# baseline (speedup 1.0000x reference)
import math
import numpy as np
from types import SimpleNamespace
from contextlib import ExitStack

import concourse.bass as bass
import concourse.mybir as mybir
from concourse.bass_utils import run_bass_kernel_spmd

F32 = mybir.dt.float32
BF16 = mybir.dt.bfloat16
AF = mybir.ActivationFunctionType
ALU = mybir.AluOpType
AX = mybir.AxisListType

CSHIFT = 12.0
ISQ = 1.0 / math.sqrt(128.0)
LN_EPS = 1e-5

FULL = dict(T=2048, E=4096, HKV=8, GQ=4)


class Sem:
    def __init__(self, name, dma=False):
        self.name, self.n, self.h, self.dma = name, 0, None, dma

    def bump(self, k=1):
        self.n += k
        return self.n


class Prog:
    ENGS = ("sync", "tensor", "vector", "scalar", "gpsimd")

    def __init__(self):
        self.st = {k: [] for k in self.ENGS}
        self.ninstr = 0
        self.hiwater = {}

    def _emit_wait(self, eng, sem, val):
        self.st[eng].append(lambda e, s=sem, v=val: e.wait_ge(s.h, v))
        self.ninstr += 1

    def waits(self, eng, waits):
        for wspec in waits:
            if len(wspec) == 3:
                sem, val, exact = wspec
                if val > 0:
                    self._emit_wait(eng, sem, val)
                continue
            sem, val = wspec
            if sem.dma:
                val = sem.n if val > 0 else 0
            if val > 0:
                self._emit_wait(eng, sem, val)

    def op(self, eng, fn, waits=(), inc=None, inck=1):
        self.waits(eng, waits)
        self.ninstr += 1
        if inc is not None:
            val = inc.bump(inck)
            self.st[eng].append(lambda e, f=fn, s=inc, k=inck: f(e).then_inc(s.h, k))
            return val
        self.st[eng].append(fn)
        return None

    def replay(self, block):
        for name in self.ENGS:
            ops = self.st[name]

            def mk(ops):
                def run(eng):
                    for fn in ops:
                        fn(eng)
                return run
            getattr(block, name)(mk(ops))
        self.st = {k: [] for k in self.ENGS}


def build_nc(cfg):
    T, E, HKV, GQ = cfg["T"], cfg["E"], cfg["HKV"], cfg["GQ"]
    NEC, TBL, TCH, NJ = E // 128, T // 128, T // 512, T // 128
    assert GQ * HKV * 128 == E and T % 512 == 0 and E % 256 == 0

    nc = bass.Bass("TRN2", target_bir_lowering=False)
    dtt = nc.dram_tensor
    ns = SimpleNamespace(cfg=cfg, NEC=NEC, TBL=TBL, TCH=TCH, NJ=NJ, T=T, E=E,
                         HKV=HKV, GQ=GQ)
    ns.x_tok = dtt("x_tok", [T, E], F32, kind="ExternalInput").ap()
    ns.xT = dtt("xT", [E, T], F32, kind="ExternalInput").ap()
    ns.wqT = dtt("wqT", [E, E], F32, kind="ExternalInput").ap()
    ns.wkT = dtt("wkT", [E, HKV * 128], F32, kind="ExternalInput").ap()
    ns.wvT = dtt("wvT", [E, HKV * 128], F32, kind="ExternalInput").ap()
    ns.woT = dtt("woT", [E, E], F32, kind="ExternalInput").ap()
    ns.acatT = dtt("acatT", [E, 128], F32, kind="ExternalInput").ap()
    ns.bcat = dtt("bcat", [128, E], F32, kind="ExternalInput").ap()
    ns.wrT = dtt("wrT", [E, 8], F32, kind="ExternalInput").ap()
    ns.brh = dtt("br", [8, 1], F32, kind="ExternalInput").ap()
    ns.cos_hi = dtt("cos_hi", [128, T], F32, kind="ExternalInput").ap()
    ns.cos_lo = dtt("cos_lo", [128, T], F32, kind="ExternalInput").ap()
    ns.sin_hi = dtt("sin_hi", [128, T], F32, kind="ExternalInput").ap()
    ns.sin_lo = dtt("sin_lo", [128, T], F32, kind="ExternalInput").ap()
    ns.identf = dtt("identf", [128, 128], F32, kind="ExternalInput").ap()
    ns.cmaskf = dtt("cmaskf", [128, 128], F32, kind="ExternalInput").ap()
    ns.eexpf = dtt("eexpf", [8, 128], F32, kind="ExternalInput").ap()
    ns.ctxs = dtt("ctxs", [E, T], BF16, kind="Internal").ap()
    ns.outT = dtt("outT", [E, T], F32, kind="ExternalOutput").ap()

    p = Prog()
    ns.S_T, ns.S_V, ns.S_S, ns.S_G = Sem("sT"), Sem("sV"), Sem("sS"), Sem("sG")
    ns.D_W = Sem("dW", dma=True)
    ns.D_W2 = Sem("dW2", dma=True)
    ns.D_XA = Sem("dXA", dma=True)
    ns.D_XB = Sem("dXB", dma=True)
    ns.D_XC = Sem("dXC", dma=True)
    ns.D_XD = Sem("dXD", dma=True)
    ns.D_C = Sem("dC", dma=True)
    ns.D_BIG = Sem("dBIG", dma=True)
    ns.D_B2 = [Sem("dB2_%d" % i, dma=True) for i in range(4)]
    ns.D_X3 = [Sem("dX3_%d" % i, dma=True) for i in range(4)]
    ns.D_GP = [Sem("dGP_%d" % i, dma=True) for i in range(2)]
    ns.D_OUT = Sem("dOUT", dma=True)
    sems = [ns.S_T, ns.S_V, ns.S_S, ns.S_G, ns.D_W, ns.D_W2, ns.D_XA, ns.D_XB,
            ns.D_XC, ns.D_XD, ns.D_C, ns.D_BIG, ns.D_OUT] + ns.D_B2 + ns.D_X3 \
        + ns.D_GP

    with ExitStack() as ctx:
        sb = nc.sbuf_tensor
        ns.big = ctx.enter_context(sb("big", [128, NEC * T], BF16))
        ns.stA = ctx.enter_context(sb("stA", [128, 2048], F32))
        ns.stB = ctx.enter_context(sb("stB", [128, 2048], F32))
        ns.wb = [ctx.enter_context(sb("wb0", [128, E], BF16)),
                 ctx.enter_context(sb("wb1", [128, E], BF16))]
        ns.lows = ctx.enter_context(sb("lows", [128, T], BF16))
        ns.id_f = ctx.enter_context(sb("id_f", [128, 128], F32))
        ns.cm_b = ctx.enter_context(sb("cm_b", [128, 128], BF16))
        ns.ee_b = ctx.enter_context(sb("ee_b", [8, 128], BF16))
        ns.br_s = ctx.enter_context(sb("br_s", [8, 1], F32))
        ns.sml = ctx.enter_context(sb("sml", [128, 216], F32))
        ns.rsb = ctx.enter_context(sb("rsb", [128, 16], F32))
        for s in sems:
            s.h = ctx.enter_context(nc.semaphore(s.name))

        ns.stA_free = (ns.S_T, 0)
        ns.stA_free2 = (ns.S_V, 0)
        ns.stB_free = (ns.S_V, 0)
        ns.wb_free = [(ns.S_T, 0), (ns.S_T, 0)]
        ns._alt_flip = [0]
        ns.stA_free3 = (ns.S_V, 0)
        ns.wsel = 0
        ns.ctx_free = [0, 0, 0, 0]
        ns.xnT_done = 0

        with ExitStack() as cA:
            ns.cos_b = cA.enter_context(sb("cos_b", [128, T], BF16))
            ns.sin_b = cA.enter_context(sb("sin_b", [128, T], BF16))
            ns.ps_main = cA.enter_context(nc.psum_tensor("ps_main", [128, 2048], F32))
            ns.ps_ctx = [cA.enter_context(nc.psum_tensor("ps_ctx%d" % i, [128, 129], F32))
                         for i in range(4)]
            with ExitStack() as c0:
                ns.stC = c0.enter_context(sb("stC", [128, 2048], F32))
                ns.stD = c0.enter_context(sb("stD", [128, 2048], F32))
                ns.stE = c0.enter_context(sb("stE", [128, 2048], F32))
                ns.stF = c0.enter_context(sb("stF", [128, 2048], F32))
                emit_phase0(p, ns)
            ns.vT_b = cA.enter_context(sb("vT_b", [128, NJ * 129], BF16))
            ns.qk_raw = cA.enter_context(sb("qk_raw", [128, T], BF16))
            ns.q_b = cA.enter_context(sb("q_b", [128, T], BF16))
            ns.k_b = cA.enter_context(sb("k_b", [128, T], BF16))
            ns.t12 = cA.enter_context(sb("t12", [128, max(T, NEC * 8)], BF16))
            ns.pT = cA.enter_context(sb("pT", [128, 3072], BF16))
            ns.ctst = cA.enter_context(sb("ctst", [128, max(T // 2, 512)], BF16))
            ns.ctok = cA.enter_context(sb("ctok", [128, 512], F32))
            blkA = cA.enter_context(nc.Block("A"))
            emit_phase1(p, ns)
            emit_phase2(p, ns)
            p.replay(blkA)

        with ExitStack() as cB:
            ns.xTb = cB.enter_context(sb("xTb", [128, 2 * T], F32))
            ns.outb = cB.enter_context(sb("outb", [128, T], F32))
            ns.bc_b = cB.enter_context(sb("bc_b", [128, E], BF16))
            ns.ps_B = cB.enter_context(nc.psum_tensor("ps_B", [128, 2048], F32))
            ns.ps_W = cB.enter_context(nc.psum_tensor("ps_W", [128, 2048], F32))
            blkB = cB.enter_context(nc.Block("B"))
            emit_phase3(p, ns)
            p.replay(blkB)

    nc._ninstr = p.ninstr
    return nc


def load_weight_cols(p, ns, dram_slice, wbuf, n_rows, wb_free, alt=False,
                     conv_eng="vector", c_base=0):
    done, out, first = 0, 0, True
    while done < n_rows:
        rows = min(2048, n_rows - done)
        nch = rows // 128
        stg, freekey, dsem = (ns.stB, "stB_free", ns.D_W)
        if alt and ns._alt_flip[0]:
            stg, freekey, dsem = (ns.stA, "stA_free3", ns.D_W2)
        if alt:
            ns._alt_flip[0] ^= 1
        src = dram_slice[done:done + rows, :].rearrange("(c p) m -> p c m", p=128)
        dst = stg.rearrange("p (c m) -> p c m", m=128)[:, 0:nch, :]
        dval = p.op("sync", lambda e, o=dst, i=src: e.dma_start(out=o, in_=i),
                    waits=[getattr(ns, freekey)], inc=dsem, inck=16)
        c0 = c_base + done // 128
        w = [(dsem, dval)]
        if first:
            w.append(wb_free)
            first = False
        csem = ns.S_G if conv_eng == "gpsimd" else ns.S_V
        out = p.op(conv_eng,
                   lambda e, o=wbuf[:, c0 * 128:(c0 + nch) * 128], i=stg[:, 0:rows]:
                   e.tensor_copy(o, i), waits=w, inc=csem)
        setattr(ns, freekey, (csem, out))
        done += rows
    return (csem, out)


def emit_phase0(p, ns):
    T, E, NEC, TBL = ns.T, ns.E, ns.NEC, ns.TBL
    S_T, S_V, S_S, S_G = ns.S_T, ns.S_V, ns.S_S, ns.S_G
    D_C, D_XA, D_XB = ns.D_C, ns.D_XA, ns.D_XB
    stA, stB, big, sml = ns.stA, ns.stB, ns.big, ns.sml

    p.op("vector", lambda e: e.memset(sml[:, 53:54], LN_EPS))
    p.op("vector", lambda e: e.memset(sml[:, 54:55], -CSHIFT), inc=S_V)
    cstate = {}

    def emit_const_dmas():
        cstate["id"] = p.op("sync", lambda e:
                            e.dma_start(out=ns.id_f[:], in_=ns.identf[:, :]),
                            inc=D_C, inck=16)
        cstate["cm"] = p.op("sync", lambda e:
                            e.dma_start(out=stA[:, 0:128], in_=ns.cmaskf[:, :]),
                            inc=D_C, inck=16)
        cstate["ee"] = p.op("sync", lambda e:
                            e.dma_start(out=stA[0:8, 128:256], in_=ns.eexpf[:, :]),
                            inc=D_C, inck=16)
        p.op("sync", lambda e: e.dma_start(out=ns.br_s[:], in_=ns.brh[:, :]),
             inc=D_C, inck=16)

    def emit_const_convs():
        p.op("vector", lambda e: e.tensor_copy(ns.cm_b[:], stA[:, 0:128]),
             waits=[(D_C, cstate["cm"])], inc=S_V)
        v_ee = p.op("vector", lambda e: e.tensor_copy(ns.ee_b[:], stA[0:8, 128:256]),
                    waits=[(D_C, cstate["ee"])], inc=S_V)
        ns.stA_free2 = (S_V, v_ee)

    tstate = {}

    def tables_dma(hi, lo):
        dh = p.op("sync", lambda e, i=hi: e.dma_start(out=stB[:, 0:T], in_=i[:, :]),
                  waits=[ns.stB_free], inc=ns.D_W, inck=16)
        dl = p.op("sync", lambda e, i=lo: e.dma_start(out=stA[:, 0:T], in_=i[:, :]),
                  waits=[ns.stA_free2], inc=ns.D_W2, inck=16)
        return dh, dl

    def tables_add():
        dh, dl = tstate["dma"]
        vad = p.op("vector", lambda e: e.tensor_add(stB[:, 0:T], stB[:, 0:T], stA[:, 0:T]),
                   waits=[(ns.D_W, dh), (ns.D_W2, dl)], inc=S_V)
        ns.stA_free2 = (S_V, vad)
        tstate["add"] = vad

    def tables_copy(dstb):
        vc = p.op("vector", lambda e, o=dstb: e.tensor_copy(o[:, 0:T], stB[:, 0:T]),
                  waits=[(S_V, tstate["add"])], inc=S_V)
        ns.stB_free = (S_V, vc)

    def tp0():
        tables_add()

    def tp1():
        tables_copy(ns.cos_b)
        tstate["dma"] = tables_dma(ns.sin_hi, ns.sin_lo)

    def tp2():
        tables_add()

    def tp3():
        tables_copy(ns.sin_b)
        q_free[0][:] = [ns.stA_free2, ns.stA_free2, ns.stB_free, ns.stB_free]
    tparts = [tp0, tp1, tp2, tp3]

    QL = E // 4
    nbc = max(1, QL // 512)
    bnw = min(512, QL)
    ncq = NEC // 4
    DXQ = [[ns.D_XA, ns.D_XB, ns.D_XC, ns.D_XD], ns.D_B2, ns.D_X3]
    qbuf = [[(stA, 0), (stA, QL), (stB, 0), (stB, QL)],
            [(ns.stC, 0), (ns.stC, QL), (ns.stD, 0), (ns.stD, QL)],
            [(ns.stE, 0), (ns.stE, QL), (ns.stF, 0), (ns.stF, QL)]]
    q_free = [[ns.stA_free2, ns.stA_free2, ns.stB_free, ns.stB_free],
              [(S_V, 0)] * 4, [(S_V, 0)] * 4]
    SSET = [(0, 48, 50, 51, 52), (96, 144, 146, 147, 148),
            (160, 208, 210, 211, 212)]
    NSL = 8
    tr_evac = [(S_V, 0)] * NSL

    def slot_ap(sl):
        if sl < 4:
            return ns.ps_main[:, sl * 512:sl * 512 + 128]
        return ns.ps_ctx[sl - 4][:, 0:128]

    def pari(i):
        return (i % 2) + 1 if i < 6 else i % 3

    def stage_dma(tb):
        pp = pari(tb)
        r = tb * 128
        dq = []
        for qq in range(4):
            buf, off = qbuf[pp][qq]
            w = [q_free[pp][qq]]
            dq.append(p.op("sync", lambda e, r=r, b=buf, o=off, qq=qq:
                           e.dma_start(out=b[:, o:o + QL],
                                       in_=ns.x_tok[r:r + 128, qq * QL:(qq + 1) * QL]),
                           waits=w, inc=DXQ[pp][qq], inck=16))
        return dq

    def stage_stats_bn(tb, dq):
        pp = pari(tb)
        sb0, agg, std, rstd, nmr = SSET[pp]
        vbs = 0
        for qq in range(4):
            buf, off = qbuf[pp][qq]
            for q in range(nbc):
                k6 = sb0 + (qq * nbc + q) * 6
                vbs = p.op("vector", lambda e, b=buf, o=off, q=q, k6=k6:
                           e.bn_stats(sml[:, k6:k6 + 6],
                                      b[:, o + q * bnw:o + (q + 1) * bnw]),
                           waits=[(DXQ[pp][qq], dq[qq])] if q == 0 else (),
                           inc=(S_V if (qq == 3 and q == nbc - 1) else None))
        v3 = p.op("vector", lambda e, a=agg, b0=sb0, n=4 * nbc * 6:
                  e.bn_aggr(sml[:, a:a + 2], sml[:, b0:b0 + n]),
                  waits=[(S_V, vbs)], inc=S_V)
        return v3

    def stage_stats_chain(tb, v3):
        pp = pari(tb)
        _sb0, agg, std, rstd, nmr = SSET[pp]
        s1 = p.op("scalar", lambda e, a=agg, s=std:
                  e.activation(sml[:, s:s + 1], sml[:, a + 1:a + 2],
                               AF.Sqrt, bias=sml[:, 53:54], scale=1.0),
                  waits=[(S_V, v3)], inc=S_S)
        vrc0 = p.op("vector", lambda e, s=std, rs=rstd:
                    e.reciprocal(sml[:, rs:rs + 1], sml[:, s:s + 1]),
                    waits=[(S_S, s1)], inc=S_V)
        v5 = p.op("vector", lambda e, a=agg, rs=rstd, nm=nmr:
                  e.scalar_tensor_tensor(
                      sml[:, nm:nm + 1], sml[:, a:a + 1], -1.0,
                      sml[:, rs:rs + 1], ALU.mult, ALU.mult),
                  waits=[(S_V, vrc0)], inc=S_V)
        return v5

    def stage_affine(tb, v5):
        pp = pari(tb)
        _sb0, _agg, _std, rstd, nmr = SSET[pp]
        sqs = []
        for qq in range(4):
            buf, off = qbuf[pp][qq]
            eng, sem = ("vector", S_V) if qq % 2 == 0 else ("gpsimd", S_G)
            sq = p.op(eng, lambda e, b=buf, o=off, rs=rstd, nm=nmr:
                      e.tensor_scalar(b[:, o:o + QL], b[:, o:o + QL],
                                      sml[:, rs:rs + 1], sml[:, nm:nm + 1],
                                      ALU.mult, ALU.add),
                      waits=[(S_V, v5)] if qq < 2 else (), inc=sem)
            sqs.append((sem, sq))
        return sqs

    def stage_transpose(tb, sqs):
        pp = pari(tb)
        r = tb * 128
        for qq in range(4):
            buf, off = qbuf[pp][qq]
            for cc in range(ncq):
                c = qq * ncq + cc
                sl = c % NSL
                w_tr = [sqs[qq], tr_evac[sl]]
                if tb == 0 and c == 0:
                    w_tr.append((D_C, cstate["id"]))
                tv = p.op("tensor", lambda e, b=buf, c0=off + cc * 128, sl=sl:
                          e.transpose(slot_ap(sl), b[:, c0:c0 + 128],
                                      ns.id_f[:]),
                          waits=w_tr, inc=S_T)
                ev = p.op("scalar", lambda e, sl=sl, c=c, r=r:
                          e.copy(big[:, c * T + r:c * T + r + 128],
                                 slot_ap(sl)),
                          waits=[(S_T, tv)], inc=S_S)
                tr_evac[sl] = (S_S, ev)
                if cc == ncq - 1:
                    q_free[pp][qq] = (S_T, tv)

    dq = {0: stage_dma(0)}
    if TBL > 1:
        dq[1] = stage_dma(1)
    emit_const_dmas()
    v5s = {0: stage_stats_chain(0, stage_stats_bn(0, dq[0]))}
    sqs_pre = {0: stage_affine(0, v5s[0])}
    if TBL > 1:
        v5s[1] = stage_stats_chain(1, stage_stats_bn(1, dq[1]))
        sqs_pre[1] = stage_affine(1, v5s[1])
    emit_const_convs()
    tstate["dma"] = tables_dma(ns.cos_hi, ns.cos_lo)
    for tb in range(TBL):
        sqs = sqs_pre[tb] if tb in sqs_pre else stage_affine(tb, v5s[tb])
        stage_transpose(tb, sqs)
        if tb + 2 < TBL:
            dq[tb + 2] = stage_dma(tb + 2)
            v5s[tb + 2] = stage_stats_chain(tb + 2,
                                            stage_stats_bn(tb + 2, dq[tb + 2]))
        if tb < 4:
            tparts[tb]()
        ns.xnT_done2 = [(S_V, S_V.n), (S_S, S_S.n), (S_G, S_G.n)]
    ns.ctx_free0 = tr_evac[4:8]
    ns.stA_free2 = max(q_free[0][0], q_free[0][1], key=lambda t: t[1])
    ns.stB_free = max(q_free[0][2], q_free[0][3], key=lambda t: t[1])


def emit_phase1(p, ns):
    T, E, NEC, TBL, TCH = ns.T, ns.E, ns.NEC, ns.TBL, ns.TCH
    S_T, S_V, S_S = ns.S_T, ns.S_V, ns.S_S
    D_C = ns.D_C
    stA, stB, big, sml = ns.stA, ns.stB, ns.big, ns.sml
    psm, ctok = ns.ps_main, ns.ctok

    acat_c = load_weight_cols(p, ns, ns.acatT[:, 0:128], ns.wb[ns.wsel], E,
                              ns.wb_free[ns.wsel])
    acat_buf = ns.wb[ns.wsel]
    acat_sel = ns.wsel
    ns.wsel ^= 1

    src = ns.wrT.rearrange("(c p) m -> p c m", p=128)
    dst = stA.rearrange("p (c m) -> p c m", m=8)[:, 0:NEC, :]
    d_wr = p.op("sync", lambda e, o=dst, i=src: e.dma_start(out=o, in_=i),
                waits=[ns.stA_free2], inc=D_C, inck=16)
    v_wr = p.op("vector", lambda e, n=NEC * 8: e.tensor_copy(ns.t12[:, 0:n], stA[:, 0:n]),
                waits=[(D_C, d_wr)] + list(ns.xnT_done2), inc=S_V)

    last = None
    for c in range(NEC):
        for tc in range(TCH):
            w = ([(S_V, v_wr)] + ns.xnT_done2) if (c == 0 and tc == 0) else ()
            last = (c == NEC - 1 and tc == TCH - 1)
            t_lg = p.op("tensor", lambda e, c=c, tc=tc, st=(c == 0), sp=(c == NEC - 1):
                        e.matmul(psm[0:8, tc * 512:(tc + 1) * 512],
                                 ns.t12[:, c * 8:(c + 1) * 8],
                                 big[:, c * T + tc * 512:c * T + tc * 512 + 512],
                                 start=st, stop=sp),
                        waits=w, inc=(S_T if last else None))
    s_lg = p.op("vector", lambda e: e.tensor_scalar_add(stA[0:8, 0:T], psm[0:8, 0:T],
                                                        ns.br_s[0:8, 0:1]),
                waits=[(S_T, t_lg)], inc=S_V)

    tr_evac = list(ns.ctx_free0)
    last_ttk = 0
    v_sT = 0
    for tb in range(TBL):
        sl = tb % 4
        ttk = p.op("tensor", lambda e, i=sl, r=tb * 128:
                   e.transpose(ns.ps_ctx[i][:, 0:8], stA[0:8, r:r + 128], ns.id_f[0:8, 0:8]),
                   waits=[(S_V, s_lg), tr_evac[sl]], inc=S_T)
        last_ttk = ttk
        vmx = p.op("vector", lambda e, i=sl:
                   e.reduce_max(sml[:, 55:56], ns.ps_ctx[i][:, 0:8], axis=AX.X, negate=True),
                   waits=[(S_T, ttk)], inc=S_V)
        sxp = p.op("scalar", lambda e, i=sl:
                   e.activation(ctok[:, 0:8], ns.ps_ctx[i][:, 0:8], AF.Exp,
                                bias=sml[:, 55:56], scale=1.0, accum_out=sml[:, 56:57]),
                   waits=[(S_V, vmx)], inc=S_S)
        vr1 = p.op("vector", lambda e: e.reciprocal(sml[:, 57:58], sml[:, 56:57]),
                   waits=[(S_S, sxp)], inc=S_V)
        vsc = p.op("vector", lambda e: e.tensor_scalar(
            ctok[:, 8:16], ctok[:, 0:8], sml[:, 57:58], 8.0, ALU.mult, ALU.mult),
            waits=[(S_V, vr1)], inc=S_V)
        ttb = p.op("tensor", lambda e, i=sl:
                   e.transpose(ns.ps_ctx[i][0:8, 0:128], ctok[:, 8:16], ns.id_f[:]),
                   waits=[(S_V, vsc)], inc=S_T)
        v_sT = p.op("vector", lambda e, i=sl, r=tb * 128:
                    e.tensor_copy(stB[0:8, r:r + 128], ns.ps_ctx[i][0:8, 0:128]),
                    waits=[(S_T, ttb)] + ([ns.stB_free] if tb == 0 else []), inc=S_V)
        tr_evac[sl] = (S_V, v_sT)
    ns.ctx_free = [v for (_s, v) in tr_evac]

    v_scb = p.op("vector", lambda e: e.tensor_copy(ns.qk_raw[0:8, 0:T], stB[0:8, 0:T]),
                 waits=[(S_V, v_sT)], inc=S_V)
    ns.stB_free = (S_V, v_scb)
    for tc in range(TCH):
        w = [(S_V, v_scb), (S_V, s_lg)] if tc == 0 else ()
        t_ex = p.op("tensor", lambda e, tc=tc:
                    e.matmul(psm[:, tc * 512:(tc + 1) * 512], ns.ee_b[:, 0:128],
                             ns.qk_raw[0:8, tc * 512:(tc + 1) * 512],
                             start=True, stop=True),
                    waits=w, inc=(S_T if tc == TCH - 1 else None))
    v_scE = p.op("vector", lambda e: e.tensor_copy(stA[:, 0:T], psm[:, 0:T]),
                 waits=[(S_T, t_ex), (S_T, last_ttk)], inc=S_V)

    for c in range(NEC):
        for tc in range(TCH):
            w = [acat_c, (S_V, v_scE)] if (c == 0 and tc == 0) else ()
            t_low = p.op("tensor", lambda e, c=c, tc=tc, st=(c == 0), sp=(c == NEC - 1):
                         e.matmul(psm[:, tc * 512:(tc + 1) * 512],
                                  acat_buf[:, c * 128:(c + 1) * 128],
                                  big[:, c * T + tc * 512:c * T + tc * 512 + 512],
                                  start=st, stop=sp),
                         waits=w, inc=(S_T if (c == NEC - 1 and tc == TCH - 1) else None))
    ns.wb_free[acat_sel] = (S_T, t_low)
    for tc in range(TCH):
        v_lo = p.op("vector", lambda e, tc=tc: e.tensor_tensor(
            ns.lows[:, tc * 512:(tc + 1) * 512], psm[:, tc * 512:(tc + 1) * 512],
            stA[:, tc * 512:(tc + 1) * 512], ALU.mult),
            waits=[(S_T, t_low)] if tc == 0 else (), inc=(S_V if tc == TCH - 1 else None))
    ns.stA_free2 = (S_V, v_lo)
    ns.psm_free = (S_V, v_lo)


def emit_phase2(p, ns):
    T, E, NEC, TCH, NJ, HKV, GQ = ns.T, ns.E, ns.NEC, ns.TCH, ns.NJ, ns.HKV, ns.GQ
    S_T, S_V, S_S, S_G = ns.S_T, ns.S_V, ns.S_S, ns.S_G
    D_OUT = ns.D_OUT
    big, psm = ns.big, ns.ps_main
    qk, qb, kb, vT, t12, pT = ns.qk_raw, ns.q_b, ns.k_b, ns.vT_b, ns.t12, ns.pT
    cosb, sinb = ns.cos_b, ns.sin_b

    vtv = vT.rearrange("p (j c) -> p j c", c=129)[:, :, 128:129]
    p.op("gpsimd", lambda e, o=vtv: e.memset(o, 1.0),
         waits=list(ns.xnT_done2), inc=S_G)

    ns.proj_gate = {}
    qkraw_free = [0, 0]
    qb_free = [0, 0]
    kb_free = 0
    vt_free = 0
    ctok_free = [0, 0, 0, 0]
    ctst_free = 0
    pending = []
    pt_free = [0] * 6

    def proj(wb_, nout, conv_val, extra=(), tcs=None, gates=None):
        if tcs is None:
            tcs = list(range(TCH))
        lastv = 0
        for c in range(NEC):
            for tc in tcs:
                w = []
                if c == 0 and tc == tcs[0]:
                    w = [conv_val, ns.psm_free] + list(extra)
                gate = gates is not None and tc == tcs[-1] and c in gates
                lastv = p.op("tensor", lambda e, c=c, tc=tc, wb=wb_,
                             st=(c == 0), sp=(c == NEC - 1):
                             e.matmul(psm[:, tc * 512:(tc + 1) * 512],
                                      wb[:, c * 128:(c + 1) * 128],
                                      big[:, c * T + tc * 512:c * T + tc * 512 + 512],
                                      start=st, stop=sp),
                             waits=w,
                             inc=(S_T if (gate or (c == NEC - 1 and tc == tcs[-1]))
                                  else None))
                if gate:
                    ns.proj_gate[c] = lastv
        return lastv

    def evac_bf16(dst, tproj, extra=(), tcs=None):
        if tcs is None:
            tcs = list(range(TCH))
        lastv = 0
        for tc in tcs:
            w = [(S_T, tproj)] + list(extra) if tc == tcs[0] else ()
            lastv = p.op("scalar", lambda e, tc=tc, d=dst:
                         e.copy(d[:, tc * 512:(tc + 1) * 512], psm[:, tc * 512:(tc + 1) * 512]),
                         waits=w, inc=(S_S if tc == tcs[-1] else None))
        return lastv

    def rope(dst, war_free, c0=0, c1=None):
        if c1 is None:
            c1 = T
        p.op("vector", lambda e: e.tensor_tensor(
            t12[0:64, c0:c1], qk[64:128, c0:c1], sinb[64:128, c0:c1], ALU.mult))
        p.op("vector", lambda e: e.tensor_tensor(
            t12[64:128, c0:c1], qk[0:64, c0:c1], sinb[0:64, c0:c1], ALU.mult))
        p.op("vector", lambda e, d=dst: e.tensor_tensor(
            d[0:64, c0:c1], qk[0:64, c0:c1], cosb[0:64, c0:c1], ALU.mult),
            waits=[(S_T, war_free)] if war_free else ())
        v4 = p.op("vector", lambda e, d=dst: e.tensor_tensor(
            d[64:128, c0:c1], qk[64:128, c0:c1], cosb[64:128, c0:c1], ALU.mult),
            inc=S_V)
        p.op("vector", lambda e, d=dst: e.tensor_sub(
            d[0:64, c0:c1], d[0:64, c0:c1], t12[0:64, c0:c1]), waits=[(S_V, v4)])
        return p.op("vector", lambda e, d=dst: e.tensor_add(
            d[64:128, c0:c1], d[64:128, c0:c1], t12[64:128, c0:c1]), inc=S_V)

    HB = max(ns.TBL // 2, 1)

    def flush_pending(nmax=None):
        nonlocal ctst_free
        cnt = 0
        while pending and (nmax is None or cnt < nmax):
            tqb, tqB, ssc_val, first_half = pending.pop(0)
            ttc = p.op("tensor", lambda e, i=tqb, s=tqb:
                       e.transpose(ns.ps_ctx[i][:, 0:128], ns.ctok[:, s * 128:(s + 1) * 128],
                                   ns.id_f[:]),
                       waits=[(S_S, ssc_val)], inc=S_T)
            ctok_free[tqb] = ttc
            w = [(S_T, ttc)]
            if tqB % HB == 0 and ctst_free:
                w.append((D_OUT, ctst_free))
            vce = p.op("vector", lambda e, i=tqb, c0=(tqB % HB) * 128:
                       e.tensor_copy(ns.ctst[:, c0:c0 + 128], ns.ps_ctx[i][:, 0:128]),
                       waits=w, inc=S_V)
            ns.ctx_free[tqb] = vce
            if tqB == HB - 1 and ns.TBL > HB:
                ctst_free = p.op("scalar", lambda e, r=ns.cur_hg * 128, hw=HB * 128:
                                 e.dma_start(out=ns.ctxs[r:r + 128, 0:hw],
                                             in_=ns.ctst[:, 0:hw]),
                                 waits=[(S_V, vce)], inc=D_OUT, inck=16)
            cnt += 1

    wk_c = [None] * HKV
    wv_c = [None] * HKV
    wq_c = {}
    wk_buf = [None] * HKV
    wv_buf = [None] * HKV
    wq_buf = {}

    def load_slice(dram, col0, tag):
        sel = ns.wsel
        ns.wsel ^= 1
        cval = load_weight_cols(p, ns, dram[:, col0:col0 + 128], ns.wb[sel], E,
                                ns.wb_free[sel])
        return cval, sel

    def load_slice_part(dram, col0, part, sel):
        r0 = part * (E // 2)
        return load_weight_cols(p, ns, dram[r0:r0 + E // 2, col0:col0 + 128],
                                ns.wb[sel], E // 2, ns.wb_free[sel],
                                c_base=part * (E // 256))

    wk_c[0], s0 = load_slice(ns.wkT, 0, "wk0")
    wk_buf[0] = s0
    wv_c[0], s1 = load_slice(ns.wvT, 0, "wv0")
    wv_buf[0] = s1

    for h in range(HKV):
        tkp = proj(ns.wb[wk_buf[h]], T, wk_c[h])
        ns.wb_free[wk_buf[h]] = (S_T, tkp)
        skv = evac_bf16(qk, tkp, extra=([(S_V, max(qkraw_free))]
                                        if max(qkraw_free) else []))
        ns.psm_free = (S_S, skv)
        wq_c[(h, 0)], wq_buf[(h, 0)] = load_slice(ns.wqT, (h * GQ) * 128, "wq")
        p.waits("vector", [(S_S, skv)] + ([(S_T, kb_free)] if kb_free else []))
        krope = rope(kb, 0)
        qkraw_free = [krope, krope]

        tvp = proj(ns.wb[wv_buf[h]], T, wv_c[h])
        ns.wb_free[wv_buf[h]] = (S_T, tvp)
        svv = 0
        for tc in range(TCH):
            w = [(S_T, tvp)] + ([ns.stA_free2] if tc == 0 else [])
            svv = p.op("scalar", lambda e, tc=tc:
                       e.activation(ns.stA[:, tc * 512:(tc + 1) * 512],
                                    psm[:, tc * 512:(tc + 1) * 512], AF.Copy,
                                    bias=0.0, scale=1.0),
                       waits=w if tc == 0 else (), inc=(S_S if tc == TCH - 1 else None))
        ns.psm_free = (S_S, svv)
        flush_pending()
        last_vtr = 0
        for j in range(NJ):
            sl = j % 4
            w = [(S_V, ns.ctx_free[sl])]
            if j == 0:
                w.append((S_S, svv))
            ttv = p.op("tensor", lambda e, j=j, i=sl:
                       e.transpose(ns.ps_ctx[i][:, 0:128], ns.stA[:, j * 128:(j + 1) * 128],
                                   ns.id_f[:]),
                       waits=w, inc=S_T)
            if j == NJ - 1:
                ns.stA_free2 = (S_T, ttv)
            w2 = [(S_T, ttv)]
            if j == 0 and vt_free:
                w2.append((S_T, vt_free))
            vev = p.op("vector", lambda e, j=j, i=sl:
                       e.tensor_copy(vT[:, j * 129:j * 129 + 128], ns.ps_ctx[i][:, 0:128]),
                       waits=w2, inc=S_V)
            ns.ctx_free[sl] = vev
            last_vtr = vev

        for g in range(GQ):
            hg = h * GQ + g
            ns.cur_hg = hg
            if h == HKV - 1 and g == GQ - 1:
                ns.strip_prev = ctst_free
            hsplit = (TCH + 1) // 2
            h_tcs = [list(range(0, hsplit)), list(range(hsplit, TCH))]
            sqes, vqrs = [], []
            tqp = 0
            final_g = (h == HKV - 1 and g == GQ - 1)
            lasthh = 1 if h_tcs[1] else 0
            NSg = HKV * GQ
            bndg = [round(i * (NSg - 1) / 4) for i in range(5)]
            ns.gate_chunks = [min(bndg[k + 1] - 1, NEC - 1) for k in range(4)]
            for hh, tcs in enumerate(h_tcs):
                if not tcs:
                    sqes.append(sqes[-1])
                    vqrs.append(vqrs[-1])
                    continue
                c0, c1 = tcs[0] * 512, (tcs[-1] + 1) * 512
                gts = set(ns.gate_chunks) if (final_g and hh == lasthh) else None
                tqp = proj(ns.wb[wq_buf[(h, g)]], T, wq_c[(h, g)], tcs=tcs,
                           gates=gts)
                wex = [(S_V, qkraw_free[hh])] if qkraw_free[hh] else []
                sq = evac_bf16(qk, tqp, extra=wex, tcs=tcs)
                p.waits("vector", [(S_S, sq)])
                vq = rope(qb, qb_free[hh], c0, c1)
                sqes.append(sq)
                vqrs.append(vq)
                qkraw_free[hh] = vq
            ns.wb_free[wq_buf[(h, g)]] = (S_T, tqp)
            if h == HKV - 1 and g == GQ - 1:
                ns.last_proj = tqp
            half_free = [sqes[0], sqes[1]]

            allg = []
            for tqc in range(TCH):
                njj = 4 * tqc + 4
                gs = [list(range(j0, min(j0 + 2, njj))) for j0 in range(0, njj, 2)]
                for gi, js in enumerate(gs):
                    allg.append((tqc, js, gi == 0, gi == len(gs) - 1))
            stop_val = [0, 0, 0, 0]
            mvals_all = {}
            sxp_all = {}
            vqr_waited = [False, False]
            LAG = 2
            for n in range(len(allg) + LAG):
                trig0 = min(1, len(allg) - 1)
                trig1 = min(6, len(allg) - 1)
                parts_now = []
                if n == trig0:
                    parts_now.append(0)
                    if trig1 == trig0:
                        parts_now.append(1)
                elif n == trig1:
                    parts_now.append(1)
                for part in parts_now:
                    if g + 1 < GQ:
                        if part == 0:
                            wq_buf[(h, g + 1)] = ns.wsel
                            ns.wsel ^= 1
                        wq_c[(h, g + 1)] = load_slice_part(
                            ns.wqT, (h * GQ + g + 1) * 128, part,
                            wq_buf[(h, g + 1)])
                    elif h + 1 < HKV:
                        if part == 0:
                            wk_buf[h + 1] = ns.wsel
                            ns.wsel ^= 1
                        wk_c[h + 1] = load_slice_part(
                            ns.wkT, (h + 1) * 128, part, wk_buf[h + 1])
                if n == min(12, len(allg)) and g == GQ - 1 and h + 1 < HKV:
                    wv_c[h + 1], wv_buf[h + 1] = \
                        load_slice(ns.wvT, (h + 1) * 128, "wv")
                if n < len(allg):
                    tqc, js, _, is_last = allg[n]
                    hf = n % 2
                    need = 0 if tqc < hsplit else 1
                    last_sc = 0
                    for idx, j in enumerate(js):
                        w = []
                        if idx == 0:
                            w = [(S_S, half_free[hf])]
                            if not vqr_waited[need]:
                                w.append((S_V, vqrs[need]))
                                vqr_waited[need] = True
                        last_sc = p.op("tensor", lambda e, j=j, hf=hf, idx=idx, tqc=tqc:
                                       e.matmul(psm[:, hf * 1024 + idx * 512:
                                                    hf * 1024 + (idx + 1) * 512],
                                                kb[:, j * 128:(j + 1) * 128],
                                                qb[:, tqc * 512:(tqc + 1) * 512],
                                                start=True, stop=True),
                                       waits=w,
                                       inc=(S_T if idx == len(js) - 1 else None))
                    if tqc == hsplit - 1 and is_last:
                        qb_free[0] = last_sc
                    if n == len(allg) - 1:
                        qb_free[1] = last_sc
                        if qb_free[0] == 0:
                            qb_free[0] = last_sc
                        if g == GQ - 1:
                            kb_free = last_sc
                    wex = [(S_T, last_sc)]
                    for idx in range(len(js)):
                        if pt_free[(n % 3) * 2 + idx]:
                            wex.append((S_T, pt_free[(n % 3) * 2 + idx]))
                    nsec = len(js)
                    trim = min((j % 4) * 128 if j // 4 == tqc else 0 for j in js)
                    sec = n % 3
                    sxp = p.op("scalar", lambda e, hf=hf, sc=sec, nn=nsec, t0=trim:
                               e.activation(pT[:, sc * 1024 + t0:sc * 1024 + nn * 512],
                                            psm[:, hf * 1024 + t0:hf * 1024 + nn * 512],
                                            AF.Exp, bias=ns.sml[:, 54:55], scale=ISQ),
                               waits=wex, inc=S_S)
                    half_free[hf] = sxp
                    sxp_all[n] = sxp
                    for idx, j in enumerate(js):
                        if j // 4 == tqc:
                            o = (n % 3) * 1024 + idx * 512 + (j % 4) * 128
                            mvals_all[(n, j)] = p.op(
                                "gpsimd", lambda e, o=o:
                                e.tensor_tensor(pT[:, o:o + 128], pT[:, o:o + 128],
                                                ns.cm_b[:], ALU.mult),
                                waits=[(S_S, sxp)], inc=S_G)
                if n >= LAG:
                    m = n - LAG
                    tqc0, js0, first0, last0 = allg[m]
                    hf0 = m % 3
                    if first0:
                        flush_pending()
                    pv_first = True
                    for idx, j in enumerate(js0):
                        tqb_lo = max(0, j - 4 * tqc0)
                        for tqb in range(tqb_lo, 4):
                            tqB = 4 * tqc0 + tqb
                            w = []
                            if tqb == tqb_lo and ((m, j) in mvals_all or pv_first):
                                w.append((S_G, mvals_all[(m, j)])
                                         if (m, j) in mvals_all
                                         else (S_S, sxp_all[m]))
                            if j == 0 and pv_first:
                                w.append((S_V, max(ns.ctx_free)))
                            pv_first = False
                            do_inc = (j == tqB) or (tqb == 3)
                            val = p.op("tensor",
                                       lambda e, j=j, hf=hf0, idx=idx, tqb=tqb, tqB=tqB:
                                       e.matmul(ns.ps_ctx[tqb][:, 0:129],
                                                pT[:, hf * 1024 + idx * 512 + tqb * 128:
                                                   hf * 1024 + idx * 512 + tqb * 128 + 128],
                                                vT[:, j * 129:(j + 1) * 129],
                                                start=(j == 0), stop=(j == tqB)),
                                       waits=w, inc=(S_T if do_inc else None))
                            if j == tqB:
                                stop_val[tqb] = val
                            if tqb == 3:
                                pt_free[hf0 * 2 + idx] = val
                            if g == GQ - 1 and j == NJ - 1:
                                vt_free = val
                    if last0:
                        for tqb in range(4):
                            tqB = 4 * tqc0 + tqb
                            vrc = p.op("vector", lambda e, i=tqb, c=tqB:
                                       e.reciprocal(ns.rsb[:, c:c + 1],
                                                    ns.ps_ctx[i][:, 128:129]),
                                       waits=[(S_T, stop_val[tqb])], inc=S_V)
                            w2 = [(S_V, vrc)]
                            if ctok_free[tqb]:
                                w2.append((S_T, ctok_free[tqb]))
                            ssc = p.op("scalar", lambda e, i=tqb, s=tqb, c=tqB:
                                       e.activation(ns.ctok[:, s * 128:(s + 1) * 128],
                                                    ns.ps_ctx[i][:, 0:128], AF.Identity,
                                                    bias=0.0, scale=ns.rsb[:, c:c + 1]),
                                       waits=w2, inc=S_S)
                            pending.append((tqb, tqB, ssc, None))
            flush_pending()
            h0 = HB * 128 if ns.TBL > HB else 0
            dst = p.op("scalar", lambda e, r=hg * 128, c0=h0:
                       e.dma_start(out=ns.ctxs[r:r + 128, c0:T],
                                   in_=ns.ctst[:, 0:T - h0]),
                       waits=[(S_V, max(ns.ctx_free))], inc=D_OUT, inck=16)
            ctst_free = dst
            ns.psm_free = (S_S, max(half_free))
    NS = HKV * GQ
    bnd = [round(i * (NS - 1) / 4) for i in range(5)]
    ns.big_bnd = bnd
    ns.big_piece = []
    psem = [ns.D_B2[0], ns.D_B2[1], ns.D_B2[2], ns.D_B2[3]]
    for k in range(4):
        a, b = bnd[k], bnd[k + 1]
        if b <= a:
            ns.big_piece.append(None)
            continue
        csrc = ns.ctxs[a * 128:b * 128, :].rearrange("(c p) t -> p c t", p=128)
        cdst = big.rearrange("p (c t) -> p c t", t=T)[:, a:b, :]
        w = [(ns.D_OUT, ns.strip_prev, True),
             (S_T, ns.proj_gate[ns.gate_chunks[k]])]
        v = p.op("sync", lambda e, o=cdst, i=csrc: e.dma_start(out=o, in_=i),
                 waits=w, inc=psem[k], inck=16)
        ns.big_piece.append((psem[k], v))
    ns.wo_pre = {}
    for eb in range(2):
        sel = ns.wsel
        ns.wsel ^= 1
        cval = load_weight_cols(p, ns, ns.woT[:, eb * 128:(eb + 1) * 128],
                                ns.wb[sel], E, ns.wb_free[sel])
        ns.wo_pre[eb] = (cval, sel)
    ns.ctst_dma_final = ctst_free


def emit_phase3(p, ns):
    T, E, NEC, TCH = ns.T, ns.E, ns.NEC, ns.TCH
    S_T, S_V, S_S, S_G = ns.S_T, ns.S_V, ns.S_S, ns.S_G
    D_OUT, D_BIG, D_C = ns.D_OUT, ns.D_BIG, ns.D_C
    big, psB, psW = ns.big, ns.ps_B, ns.ps_W

    NS = E // 128
    p.waits("sync", [(D_OUT, ns.ctst_dma_final)])
    d_ctx = p.op("sync", lambda e, o=big[:, (NS - 1) * T:NS * T],
                 i=ns.ctxs[(NS - 1) * 128:NS * 128, :]:
                 e.dma_start(out=o, in_=i), inc=D_BIG, inck=16)

    def big_gate(c):
        if c >= NS - 1:
            return (D_BIG, d_ctx)
        for k in range(4):
            if c < ns.big_bnd[k + 1]:
                return ns.big_piece[k] or (D_BIG, d_ctx)
        return (D_BIG, d_ctx)
    gate_cs = sorted(set([0] + ns.big_bnd[1:4] + [NS - 1]))

    v_bc = 0
    for h0 in range(0, E, 2048):
        wdt = min(2048, E - h0)
        d = p.op("sync", lambda e, h0=h0, w=wdt:
                 e.dma_start(out=ns.stA[:, 0:w], in_=ns.bcat[:, h0:h0 + w]),
                 waits=[(S_V, v_bc)] if v_bc else [ns.stA_free2], inc=D_C, inck=16)
        v_bc = p.op("vector", lambda e, h0=h0, w=wdt:
                    e.tensor_copy(ns.bc_b[:, h0:h0 + w], ns.stA[:, 0:w]),
                    waits=[(D_C, d)], inc=S_V)
        ns.stA_free3 = (S_V, v_bc)

    wo_c = {eb: cv for eb, (cv, _s) in ns.wo_pre.items()}
    wo_buf = {eb: s for eb, (_cv, s) in ns.wo_pre.items()}

    def load_wo(eb):
        sel = ns.wsel
        ns.wsel ^= 1
        wo_c[eb] = load_weight_cols(p, ns, ns.woT[:, eb * 128:(eb + 1) * 128],
                                    ns.wb[sel], E, ns.wb_free[sel], alt=True)
        wo_buf[eb] = sel
    xT_free = [0, 0]
    psW_free = 0
    s_g_prev = 0
    d_out = 0
    ps2 = [psW, psB]
    for eb in range(NEC):
        wo_ps = ps2[eb % 2]
        b_ps = ps2[1 - eb % 2]
        xsel = eb % 2
        D_X = ns.D_XA if xsel == 0 else ns.D_XB
        d_x = p.op("sync", lambda e, r=eb * 128, s=xsel:
                   e.dma_start(out=ns.xTb[:, s * T:s * T + T], in_=ns.xT[r:r + 128, :]),
                   waits=[(S_G, xT_free[xsel])] if xT_free[xsel] else (),
                   inc=D_X, inck=16)
        t_B = 0
        for tc in range(TCH):
            w = []
            if tc == 0 and eb == 0:
                w.append((S_V, v_bc))
            if psW_free:
                w.append((S_V, psW_free[tc]))
            t_B = p.op("tensor", lambda e, eb=eb, tc=tc, ps=b_ps:
                       e.matmul(ps[:, tc * 512:(tc + 1) * 512],
                                ns.bc_b[:, eb * 128:(eb + 1) * 128],
                                ns.lows[:, tc * 512:(tc + 1) * 512],
                                start=True, stop=True),
                       waits=w, inc=(S_T if tc == TCH - 1 else None))
        s_g = 0
        for tc in range(TCH):
            w = [(S_T, t_B)]
            if tc == 0 and d_out:
                w.append((D_OUT, d_out))
            s_g = p.op("scalar", lambda e, tc=tc, ps=b_ps:
                       e.activation(ns.outb[:, tc * 512:(tc + 1) * 512],
                                    ps[:, tc * 512:(tc + 1) * 512], AF.Silu,
                                    bias=0.0, scale=2.0),
                       waits=w if tc == 0 else (), inc=(S_S if tc == TCH - 1 else None))
        t_W = 0
        wo_strip = []
        for tc in range(TCH):
            for c in range(NEC):
                w = []
                if c == 0 and tc == 0:
                    w = [wo_c[eb]]
                    if s_g_prev:
                        w.append((S_S, s_g_prev))
                if eb == 0 and tc == 0 and c in gate_cs:
                    w.append(big_gate(c))
                t_W = p.op("tensor", lambda e, c=c, tc=tc, wb=ns.wb[wo_buf[eb]],
                           ps=wo_ps, st=(c == 0), sp=(c == NEC - 1):
                           e.matmul(ps[:, tc * 512:(tc + 1) * 512],
                                    wb[:, c * 128:(c + 1) * 128],
                                    big[:, c * T + tc * 512:c * T + tc * 512 + 512],
                                    start=st, stop=sp),
                           waits=w,
                           inc=(S_T if c == NEC - 1 else None))
            wo_strip.append(t_W)
        ns.wb_free[wo_buf[eb]] = (S_T, t_W)
        v_strips = []
        v_cb = 0
        for tc in range(TCH):
            w = [(S_T, wo_strip[tc])]
            if tc == 0:
                w.append((S_S, s_g))
            v_cb = p.op("vector", lambda e, tc=tc, ps=wo_ps:
                        e.tensor_tensor(ns.outb[:, tc * 512:(tc + 1) * 512],
                                        ps[:, tc * 512:(tc + 1) * 512],
                                        ns.outb[:, tc * 512:(tc + 1) * 512], ALU.mult),
                        waits=w, inc=S_V)
            v_strips.append(v_cb)
        psW_free = v_strips
        s_g_prev = s_g
        if eb + 2 < NEC:
            load_wo(eb + 2)
        g_cb = 0
        for tc in range(TCH):
            w = [(S_V, v_cb), (D_X, d_x)] if tc == 0 else ()
            g_cb = p.op("gpsimd", lambda e, tc=tc, s=xsel:
                        e.tensor_add(ns.outb[:, tc * 512:(tc + 1) * 512],
                                     ns.outb[:, tc * 512:(tc + 1) * 512],
                                     ns.xTb[:, s * T + tc * 512:s * T + (tc + 1) * 512]),
                        waits=w, inc=(S_G if tc == TCH - 1 else None))
        xT_free[xsel] = g_cb
        d_out = p.op("scalar", lambda e, r=eb * 128:
                     e.dma_start(out=ns.outT[r:r + 128, :], in_=ns.outb[:, 0:T]),
                     waits=[(S_G, g_cb)], inc=D_OUT, inck=16)
    p.waits("scalar", [(D_OUT, d_out)])



def _rope_tables(T):
    DH, theta, factor, hi_f, lo_f, orig = 128, 500000.0, 32.0, 4.0, 1.0, 8192
    i = np.arange(0, DH, 2, dtype=np.float64)
    inv = 1.0 / theta ** (i / DH)
    wav = 2.0 * math.pi / inv
    cut_lo, cut_hi = orig / hi_f, orig / lo_f
    inv = np.where(wav > cut_hi, inv / factor, inv)
    smooth = (orig / wav - lo_f) / (hi_f - lo_f)
    smoothed = (1.0 - smooth) * inv / factor + smooth * inv
    is_med = ~(wav < cut_hi) & ~(wav > cut_lo)
    inv = np.where(is_med, smoothed, inv)
    pos = np.arange(T, dtype=np.float64)
    fr = pos[:, None] * inv[None, :]
    emb = np.concatenate([fr, fr], axis=1).T
    out = []
    for tab in (np.cos(emb), np.sin(emb)):
        hi = tab.astype(np.float32)
        lo = (tab - hi.astype(np.float64)).astype(np.float32)
        out += [np.ascontiguousarray(hi), np.ascontiguousarray(lo)]
    return out


def make_inputs(cfg, x, Wq, Wk, Wv, Wo, A, B, Wr, br):
    T, E = cfg["T"], cfg["E"]
    f = np.float32
    ch, cl, sh, sl = _rope_tables(T)
    shared = dict(
        wqT=np.ascontiguousarray(np.asarray(Wq, f).T),
        wkT=np.ascontiguousarray(np.asarray(Wk, f).T),
        wvT=np.ascontiguousarray(np.asarray(Wv, f).T),
        woT=np.ascontiguousarray(np.asarray(Wo, f).T),
        acatT=np.ascontiguousarray(np.asarray(A, f).reshape(128, E).T),
        bcat=np.ascontiguousarray(np.asarray(B, f).transpose(0, 2, 1).reshape(128, E)),
        wrT=np.ascontiguousarray(np.asarray(Wr, f).T),
        br=np.ascontiguousarray(np.asarray(br, f).reshape(8, 1)),
        cos_hi=ch, cos_lo=cl, sin_hi=sh, sin_lo=sl,
        identf=np.eye(128, dtype=f),
        cmaskf=np.ascontiguousarray(
            (np.arange(128)[:, None] <= np.arange(128)[None, :]).astype(f)),
        eexpf=np.ascontiguousarray(
            (np.arange(128)[None, :] // 16 == np.arange(8)[:, None]).astype(f)),
    )
    in_maps = []
    for b in range(x.shape[0]):
        m = dict(shared)
        m["x_tok"] = np.ascontiguousarray(np.asarray(x[b], f))
        m["xT"] = np.ascontiguousarray(np.asarray(x[b], f).T)
        in_maps.append(m)
    return in_maps


def kernel(L0_attn_x, L0_attn_Wq, L0_attn_Wk, L0_attn_Wv, L0_attn_Wo,
           L0_ffn_A, L0_ffn_B, L0_ffn_Wr, L0_ffn_br):
    cfg = FULL
    x = np.asarray(L0_attn_x, np.float32)
    in_maps = make_inputs(cfg, x, L0_attn_Wq, L0_attn_Wk, L0_attn_Wv, L0_attn_Wo,
                          L0_ffn_A, L0_ffn_B, L0_ffn_Wr, L0_ffn_br)
    nc = build_nc(cfg)
    res = run_bass_kernel_spmd(nc, in_maps, core_ids=list(range(len(in_maps))))
    out = np.stack([np.asarray(r["outT"], np.float32).T for r in res.results])
    return np.ascontiguousarray(out)


# version 27 (final)
# speedup vs baseline: 1.1018x; 1.1018x over previous
import math
import numpy as np
from types import SimpleNamespace
from contextlib import ExitStack

import concourse.bass as bass
import concourse.mybir as mybir
from concourse.bass_utils import run_bass_kernel_spmd

F32 = mybir.dt.float32
BF16 = mybir.dt.bfloat16
AF = mybir.ActivationFunctionType
ALU = mybir.AluOpType
AX = mybir.AxisListType

CSHIFT = 12.0
ISQ = 1.0 / math.sqrt(128.0)
LN_EPS = 1e-5

FULL = dict(T=2048, E=4096, HKV=8, GQ=4)


class Sem:
    def __init__(self, name, dma=False):
        self.name, self.n, self.h, self.dma = name, 0, None, dma

    def bump(self, k=1):
        self.n += k
        return self.n


class Prog:
    ENGS = ("sync", "tensor", "vector", "scalar", "gpsimd")

    def __init__(self):
        self.st = {k: [] for k in self.ENGS}
        self.ninstr = 0
        self.hiwater = {}

    def _emit_wait(self, eng, sem, val):
        self.st[eng].append(lambda e, s=sem, v=val: e.wait_ge(s.h, v))
        self.ninstr += 1

    def waits(self, eng, waits):
        for wspec in waits:
            if len(wspec) == 3:
                sem, val, exact = wspec
                if val > 0:
                    self._emit_wait(eng, sem, val)
                continue
            sem, val = wspec
            if sem.dma:
                val = sem.n if val > 0 else 0
            if val > 0:
                self._emit_wait(eng, sem, val)

    def op(self, eng, fn, waits=(), inc=None, inck=1):
        self.waits(eng, waits)
        self.ninstr += 1
        if inc is not None:
            val = inc.bump(inck)
            self.st[eng].append(lambda e, f=fn, s=inc, k=inck: f(e).then_inc(s.h, k))
            return val
        self.st[eng].append(fn)
        return None

    def replay(self, block):
        for name in self.ENGS:
            ops = self.st[name]

            def mk(ops):
                def run(eng):
                    for fn in ops:
                        fn(eng)
                return run
            getattr(block, name)(mk(ops))
        self.st = {k: [] for k in self.ENGS}


def build_nc(cfg):
    T, E, HKV, GQ = cfg["T"], cfg["E"], cfg["HKV"], cfg["GQ"]
    NEC, TBL, TCH, NJ = E // 128, T // 128, T // 512, T // 128
    assert GQ * HKV * 128 == E and T % 512 == 0 and E % 256 == 0

    nc = bass.Bass("TRN2", target_bir_lowering=False)
    dtt = nc.dram_tensor
    ns = SimpleNamespace(cfg=cfg, NEC=NEC, TBL=TBL, TCH=TCH, NJ=NJ, T=T, E=E,
                         HKV=HKV, GQ=GQ)
    ns.x_tok = dtt("x_tok", [T, E], F32, kind="ExternalInput").ap()
    ns.xT = dtt("xT", [E, T], F32, kind="ExternalInput").ap()
    ns.wqT = dtt("wqT", [E, E], F32, kind="ExternalInput").ap()
    ns.wkT = dtt("wkT", [E, HKV * 128], F32, kind="ExternalInput").ap()
    ns.wvT = dtt("wvT", [E, HKV * 128], F32, kind="ExternalInput").ap()
    ns.woT = dtt("woT", [E, E], F32, kind="ExternalInput").ap()
    ns.acatT = dtt("acatT", [E, 128], F32, kind="ExternalInput").ap()
    ns.bcat = dtt("bcat", [128, E], F32, kind="ExternalInput").ap()
    ns.wrT = dtt("wrT", [E, 8], F32, kind="ExternalInput").ap()
    ns.brh = dtt("br", [8, 1], F32, kind="ExternalInput").ap()
    ns.cos_hi = dtt("cos_hi", [128, T], F32, kind="ExternalInput").ap()
    ns.cos_lo = dtt("cos_lo", [128, T], F32, kind="ExternalInput").ap()
    ns.sin_hi = dtt("sin_hi", [128, T], F32, kind="ExternalInput").ap()
    ns.sin_lo = dtt("sin_lo", [128, T], F32, kind="ExternalInput").ap()
    ns.identf = dtt("identf", [128, 128], F32, kind="ExternalInput").ap()
    ns.cmaskf = dtt("cmaskf", [128, 128], F32, kind="ExternalInput").ap()
    ns.eexpf = dtt("eexpf", [8, 128], F32, kind="ExternalInput").ap()
    ns.ctxs = dtt("ctxs", [E, T], BF16, kind="Internal").ap()
    ns.outT = dtt("outT", [E, T], F32, kind="ExternalOutput").ap()

    p = Prog()
    ns.S_T, ns.S_V, ns.S_S, ns.S_G = Sem("sT"), Sem("sV"), Sem("sS"), Sem("sG")
    ns.D_W = Sem("dW", dma=True)
    ns.D_W2 = Sem("dW2", dma=True)
    ns.D_XA = Sem("dXA", dma=True)
    ns.D_XB = Sem("dXB", dma=True)
    ns.D_XC = Sem("dXC", dma=True)
    ns.D_XD = Sem("dXD", dma=True)
    ns.D_C = Sem("dC", dma=True)
    ns.D_BIG = Sem("dBIG", dma=True)
    ns.D_B2 = [Sem("dB2_%d" % i, dma=True) for i in range(4)]
    ns.D_X3 = [Sem("dX3_%d" % i, dma=True) for i in range(4)]
    ns.D_GP = [Sem("dGP_%d" % i, dma=True) for i in range(2)]
    ns.D_OUT = Sem("dOUT", dma=True)
    sems = [ns.S_T, ns.S_V, ns.S_S, ns.S_G, ns.D_W, ns.D_W2, ns.D_XA, ns.D_XB,
            ns.D_XC, ns.D_XD, ns.D_C, ns.D_BIG, ns.D_OUT] + ns.D_B2 + ns.D_X3 \
        + ns.D_GP

    with ExitStack() as ctx:
        sb = nc.sbuf_tensor
        ns.big = ctx.enter_context(sb("big", [128, NEC * T], BF16))
        ns.stA = ctx.enter_context(sb("stA", [128, 2048], F32))
        ns.stB = ctx.enter_context(sb("stB", [128, 2048], F32))
        ns.wb = [ctx.enter_context(sb("wb0", [128, E], BF16)),
                 ctx.enter_context(sb("wb1", [128, E], BF16))]
        ns.lows = ctx.enter_context(sb("lows", [128, T], BF16))
        ns.id_f = ctx.enter_context(sb("id_f", [128, 128], F32))
        ns.cm_b = ctx.enter_context(sb("cm_b", [128, 128], BF16))
        ns.ee_b = ctx.enter_context(sb("ee_b", [8, 128], BF16))
        ns.br_s = ctx.enter_context(sb("br_s", [8, 1], F32))
        ns.sml = ctx.enter_context(sb("sml", [128, 216], F32))
        ns.rsb = ctx.enter_context(sb("rsb", [128, 16], F32))
        for s in sems:
            s.h = ctx.enter_context(nc.semaphore(s.name))

        ns.stA_free = (ns.S_T, 0)
        ns.stA_free2 = (ns.S_V, 0)
        ns.stB_free = (ns.S_V, 0)
        ns.wb_free = [(ns.S_T, 0), (ns.S_T, 0)]
        ns._alt_flip = [0]
        ns.stA_free3 = (ns.S_V, 0)
        ns.wsel = 0
        ns.ctx_free = [0, 0, 0, 0]
        ns.xnT_done = 0

        with ExitStack() as cA:
            ns.cos_b = cA.enter_context(sb("cos_b", [128, T], BF16))
            ns.sin_b = cA.enter_context(sb("sin_b", [128, T], BF16))
            ns.ps_main = cA.enter_context(nc.psum_tensor("ps_main", [128, 2048], F32))
            ns.ps_ctx = [cA.enter_context(nc.psum_tensor("ps_ctx%d" % i, [128, 129], F32))
                         for i in range(4)]
            with ExitStack() as c0:
                ns.stC = c0.enter_context(sb("stC", [128, 2048], F32))
                ns.stD = c0.enter_context(sb("stD", [128, 2048], F32))
                ns.stE = c0.enter_context(sb("stE", [128, 2048], F32))
                ns.stF = c0.enter_context(sb("stF", [128, 2048], F32))
                emit_phase0(p, ns)
            ns.vT_b = cA.enter_context(sb("vT_b", [128, NJ * 129], BF16))
            ns.qk_raw = cA.enter_context(sb("qk_raw", [128, T], BF16))
            ns.q_b = cA.enter_context(sb("q_b", [128, T], BF16))
            ns.k_b = cA.enter_context(sb("k_b", [128, T], BF16))
            ns.t12 = cA.enter_context(sb("t12", [128, max(T, NEC * 8)], BF16))
            ns.pT = cA.enter_context(sb("pT", [128, 3072], BF16))
            ns.ctst = cA.enter_context(sb("ctst", [128, max(T // 2, 512)], BF16))
            ns.ctok = cA.enter_context(sb("ctok", [128, 512], F32))
            blkA = cA.enter_context(nc.Block("A"))
            emit_phase1(p, ns)
            emit_phase2(p, ns)
            p.replay(blkA)

        with ExitStack() as cB:
            ns.xTb = cB.enter_context(sb("xTb", [128, 2 * T], F32))
            ns.outb = cB.enter_context(sb("outb", [128, T], F32))
            ns.bc_b = cB.enter_context(sb("bc_b", [128, E], BF16))
            ns.ps_B = cB.enter_context(nc.psum_tensor("ps_B", [128, 2048], F32))
            ns.ps_W = cB.enter_context(nc.psum_tensor("ps_W", [128, 2048], F32))
            blkB = cB.enter_context(nc.Block("B"))
            emit_phase3(p, ns)
            p.replay(blkB)

    nc._ninstr = p.ninstr
    return nc


def load_weight_cols(p, ns, dram_slice, wbuf, n_rows, wb_free, alt=False,
                     conv_eng="vector", c_base=0):
    done, out, first = 0, 0, True
    while done < n_rows:
        rows = min(2048, n_rows - done)
        nch = rows // 128
        stg, freekey, dsem = (ns.stB, "stB_free", ns.D_W)
        if alt and ns._alt_flip[0]:
            stg, freekey, dsem = (ns.stA, "stA_free3", ns.D_W2)
        if alt:
            ns._alt_flip[0] ^= 1
        src = dram_slice[done:done + rows, :].rearrange("(c p) m -> p c m", p=128)
        dst = stg.rearrange("p (c m) -> p c m", m=128)[:, 0:nch, :]
        dval = p.op("sync", lambda e, o=dst, i=src: e.dma_start(out=o, in_=i),
                    waits=[getattr(ns, freekey)], inc=dsem, inck=16)
        c0 = c_base + done // 128
        w = [(dsem, dval)]
        if first:
            w.append(wb_free)
            first = False
        csem = ns.S_G if conv_eng == "gpsimd" else ns.S_V
        out = p.op(conv_eng,
                   lambda e, o=wbuf[:, c0 * 128:(c0 + nch) * 128], i=stg[:, 0:rows]:
                   e.tensor_copy(o, i), waits=w, inc=csem)
        setattr(ns, freekey, (csem, out))
        done += rows
    return (csem, out)


def emit_phase0(p, ns):
    T, E, NEC, TBL = ns.T, ns.E, ns.NEC, ns.TBL
    S_T, S_V, S_S, S_G = ns.S_T, ns.S_V, ns.S_S, ns.S_G
    D_C, D_XA, D_XB = ns.D_C, ns.D_XA, ns.D_XB
    stA, stB, big, sml = ns.stA, ns.stB, ns.big, ns.sml

    p.op("vector", lambda e: e.memset(sml[:, 53:54], LN_EPS))
    p.op("vector", lambda e: e.memset(sml[:, 54:55], -CSHIFT), inc=S_V)
    cstate = {}

    def emit_const_dmas():
        cstate["id"] = p.op("sync", lambda e:
                            e.dma_start(out=ns.id_f[:], in_=ns.identf[:, :]),
                            inc=D_C, inck=16)
        cstate["cm"] = p.op("sync", lambda e:
                            e.dma_start(out=stA[:, 0:128], in_=ns.cmaskf[:, :]),
                            inc=D_C, inck=16)
        cstate["ee"] = p.op("sync", lambda e:
                            e.dma_start(out=stA[0:8, 128:256], in_=ns.eexpf[:, :]),
                            inc=D_C, inck=16)
        p.op("sync", lambda e: e.dma_start(out=ns.br_s[:], in_=ns.brh[:, :]),
             inc=D_C, inck=16)

    def emit_const_convs():
        p.op("vector", lambda e: e.tensor_copy(ns.cm_b[:], stA[:, 0:128]),
             waits=[(D_C, cstate["cm"])], inc=S_V)
        v_ee = p.op("vector", lambda e: e.tensor_copy(ns.ee_b[:], stA[0:8, 128:256]),
                    waits=[(D_C, cstate["ee"])], inc=S_V)
        ns.stA_free2 = (S_V, v_ee)

    tstate = {}

    def tables_dma(hi, lo):
        dh = p.op("sync", lambda e, i=hi: e.dma_start(out=stB[:, 0:T], in_=i[:, :]),
                  waits=[ns.stB_free], inc=ns.D_W, inck=16)
        dl = p.op("sync", lambda e, i=lo: e.dma_start(out=stA[:, 0:T], in_=i[:, :]),
                  waits=[ns.stA_free2], inc=ns.D_W2, inck=16)
        return dh, dl

    def tables_add():
        dh, dl = tstate["dma"]
        vad = p.op("vector", lambda e: e.tensor_add(stB[:, 0:T], stB[:, 0:T], stA[:, 0:T]),
                   waits=[(ns.D_W, dh), (ns.D_W2, dl)], inc=S_V)
        ns.stA_free2 = (S_V, vad)
        tstate["add"] = vad

    def tables_copy(dstb):
        vc = p.op("vector", lambda e, o=dstb: e.tensor_copy(o[:, 0:T], stB[:, 0:T]),
                  waits=[(S_V, tstate["add"])], inc=S_V)
        ns.stB_free = (S_V, vc)

    def tp0():
        tables_add()

    def tp1():
        tables_copy(ns.cos_b)
        tstate["dma"] = tables_dma(ns.sin_hi, ns.sin_lo)

    def tp2():
        tables_add()

    def tp3():
        tables_copy(ns.sin_b)
        q_free[0][:] = [ns.stA_free2, ns.stA_free2, ns.stB_free, ns.stB_free]
    tparts = [tp0, tp1, tp2, tp3]

    QL = E // 4
    nbc = max(1, QL // 512)
    bnw = min(512, QL)
    ncq = NEC // 4
    DXQ = [[ns.D_XA, ns.D_XB, ns.D_XC, ns.D_XD], ns.D_B2, ns.D_X3]
    qbuf = [[(stA, 0), (stA, QL), (stB, 0), (stB, QL)],
            [(ns.stC, 0), (ns.stC, QL), (ns.stD, 0), (ns.stD, QL)],
            [(ns.stE, 0), (ns.stE, QL), (ns.stF, 0), (ns.stF, QL)]]
    q_free = [[ns.stA_free2, ns.stA_free2, ns.stB_free, ns.stB_free],
              [(S_V, 0)] * 4, [(S_V, 0)] * 4]
    SSET = [(0, 48, 50, 51, 52), (96, 144, 146, 147, 148),
            (160, 208, 210, 211, 212)]
    NSL = 8
    tr_evac = [(S_V, 0)] * NSL

    def slot_ap(sl):
        if sl < 4:
            return ns.ps_main[:, sl * 512:sl * 512 + 128]
        return ns.ps_ctx[sl - 4][:, 0:128]

    def pari(i):
        return (i % 2) + 1 if i < 6 else i % 3

    def stage_dma(tb):
        pp = pari(tb)
        r = tb * 128
        dq = []
        for qq in range(4):
            buf, off = qbuf[pp][qq]
            w = [q_free[pp][qq]]
            dq.append(p.op("sync", lambda e, r=r, b=buf, o=off, qq=qq:
                           e.dma_start(out=b[:, o:o + QL],
                                       in_=ns.x_tok[r:r + 128, qq * QL:(qq + 1) * QL]),
                           waits=w, inc=DXQ[pp][qq], inck=16))
        return dq

    def stage_stats_bn(tb, dq):
        pp = pari(tb)
        sb0, agg, std, rstd, nmr = SSET[pp]
        vbs = 0
        for qq in range(4):
            buf, off = qbuf[pp][qq]
            for q in range(nbc):
                k6 = sb0 + (qq * nbc + q) * 6
                vbs = p.op("vector", lambda e, b=buf, o=off, q=q, k6=k6:
                           e.bn_stats(sml[:, k6:k6 + 6],
                                      b[:, o + q * bnw:o + (q + 1) * bnw]),
                           waits=[(DXQ[pp][qq], dq[qq])] if q == 0 else (),
                           inc=(S_V if (qq == 3 and q == nbc - 1) else None))
        v3 = p.op("vector", lambda e, a=agg, b0=sb0, n=4 * nbc * 6:
                  e.bn_aggr(sml[:, a:a + 2], sml[:, b0:b0 + n]),
                  waits=[(S_V, vbs)], inc=S_V)
        return v3

    def stage_stats_chain(tb, v3):
        pp = pari(tb)
        _sb0, agg, std, rstd, nmr = SSET[pp]
        s1 = p.op("scalar", lambda e, a=agg, s=std:
                  e.activation(sml[:, s:s + 1], sml[:, a + 1:a + 2],
                               AF.Sqrt, bias=sml[:, 53:54], scale=1.0),
                  waits=[(S_V, v3)], inc=S_S)
        vrc0 = p.op("vector", lambda e, s=std, rs=rstd:
                    e.reciprocal(sml[:, rs:rs + 1], sml[:, s:s + 1]),
                    waits=[(S_S, s1)], inc=S_V)
        v5 = p.op("vector", lambda e, a=agg, rs=rstd, nm=nmr:
                  e.scalar_tensor_tensor(
                      sml[:, nm:nm + 1], sml[:, a:a + 1], -1.0,
                      sml[:, rs:rs + 1], ALU.mult, ALU.mult),
                  waits=[(S_V, vrc0)], inc=S_V)
        return v5

    def stage_affine(tb, v5):
        pp = pari(tb)
        _sb0, _agg, _std, rstd, nmr = SSET[pp]
        sqs = []
        for qq in range(4):
            buf, off = qbuf[pp][qq]
            eng, sem = ("vector", S_V) if qq % 2 == 0 else ("gpsimd", S_G)
            sq = p.op(eng, lambda e, b=buf, o=off, rs=rstd, nm=nmr:
                      e.tensor_scalar(b[:, o:o + QL], b[:, o:o + QL],
                                      sml[:, rs:rs + 1], sml[:, nm:nm + 1],
                                      ALU.mult, ALU.add),
                      waits=[(S_V, v5)] if qq < 2 else (), inc=sem)
            sqs.append((sem, sq))
        return sqs

    def stage_transpose(tb, sqs):
        pp = pari(tb)
        r = tb * 128
        for qq in range(4):
            buf, off = qbuf[pp][qq]
            for cc in range(ncq):
                c = qq * ncq + cc
                sl = c % NSL
                w_tr = [sqs[qq], tr_evac[sl]]
                if tb == 0 and c == 0:
                    w_tr.append((D_C, cstate["id"]))
                tv = p.op("tensor", lambda e, b=buf, c0=off + cc * 128, sl=sl:
                          e.transpose(slot_ap(sl), b[:, c0:c0 + 128],
                                      ns.id_f[:]),
                          waits=w_tr, inc=S_T)
                ev = p.op("scalar", lambda e, sl=sl, c=c, r=r:
                          e.copy(big[:, c * T + r:c * T + r + 128],
                                 slot_ap(sl)),
                          waits=[(S_T, tv)], inc=S_S)
                tr_evac[sl] = (S_S, ev)
                if cc == ncq - 1:
                    q_free[pp][qq] = (S_T, tv)

    dq = {0: stage_dma(0)}
    if TBL > 1:
        dq[1] = stage_dma(1)
    emit_const_dmas()
    v5s = {0: stage_stats_chain(0, stage_stats_bn(0, dq[0]))}
    sqs_pre = {0: stage_affine(0, v5s[0])}
    if TBL > 1:
        v5s[1] = stage_stats_chain(1, stage_stats_bn(1, dq[1]))
        sqs_pre[1] = stage_affine(1, v5s[1])
    emit_const_convs()
    tstate["dma"] = tables_dma(ns.cos_hi, ns.cos_lo)
    for tb in range(TBL):
        sqs = sqs_pre[tb] if tb in sqs_pre else stage_affine(tb, v5s[tb])
        stage_transpose(tb, sqs)
        if tb + 2 < TBL:
            dq[tb + 2] = stage_dma(tb + 2)
            v5s[tb + 2] = stage_stats_chain(tb + 2,
                                            stage_stats_bn(tb + 2, dq[tb + 2]))
        if tb < 4:
            tparts[tb]()
        ns.xnT_done2 = [(S_V, S_V.n), (S_S, S_S.n), (S_G, S_G.n)]
    ns.ctx_free0 = tr_evac[4:8]
    ns.stA_free2 = max(q_free[0][0], q_free[0][1], key=lambda t: t[1])
    ns.stB_free = max(q_free[0][2], q_free[0][3], key=lambda t: t[1])


def emit_phase1(p, ns):
    T, E, NEC, TBL, TCH = ns.T, ns.E, ns.NEC, ns.TBL, ns.TCH
    S_T, S_V, S_S = ns.S_T, ns.S_V, ns.S_S
    D_C = ns.D_C
    stA, stB, big, sml = ns.stA, ns.stB, ns.big, ns.sml
    psm, ctok = ns.ps_main, ns.ctok

    acat_c = load_weight_cols(p, ns, ns.acatT[:, 0:128], ns.wb[ns.wsel], E,
                              ns.wb_free[ns.wsel])
    acat_buf = ns.wb[ns.wsel]
    acat_sel = ns.wsel
    ns.wsel ^= 1

    src = ns.wrT.rearrange("(c p) m -> p c m", p=128)
    dst = stA.rearrange("p (c m) -> p c m", m=8)[:, 0:NEC, :]
    d_wr = p.op("sync", lambda e, o=dst, i=src: e.dma_start(out=o, in_=i),
                waits=[ns.stA_free2], inc=D_C, inck=16)
    v_wr = p.op("vector", lambda e, n=NEC * 8: e.tensor_copy(ns.t12[:, 0:n], stA[:, 0:n]),
                waits=[(D_C, d_wr)] + list(ns.xnT_done2), inc=S_V)

    last = None
    for c in range(NEC):
        for tc in range(TCH):
            w = ([(S_V, v_wr)] + ns.xnT_done2) if (c == 0 and tc == 0) else ()
            last = (c == NEC - 1 and tc == TCH - 1)
            t_lg = p.op("tensor", lambda e, c=c, tc=tc, st=(c == 0), sp=(c == NEC - 1):
                        e.matmul(psm[0:8, tc * 512:(tc + 1) * 512],
                                 ns.t12[:, c * 8:(c + 1) * 8],
                                 big[:, c * T + tc * 512:c * T + tc * 512 + 512],
                                 start=st, stop=sp),
                        waits=w, inc=(S_T if last else None))
    s_lg = p.op("vector", lambda e: e.tensor_scalar_add(stA[0:8, 0:T], psm[0:8, 0:T],
                                                        ns.br_s[0:8, 0:1]),
                waits=[(S_T, t_lg)], inc=S_V)

    tr_evac = list(ns.ctx_free0)
    last_ttk = 0
    v_sT = 0

    def softmax_ttk(tb):
        nonlocal last_ttk
        sl = tb % 4
        ttk = p.op("tensor", lambda e, i=sl, r=tb * 128:
                   e.transpose(ns.ps_ctx[i][:, 0:8], stA[0:8, r:r + 128], ns.id_f[0:8, 0:8]),
                   waits=[(S_V, s_lg), tr_evac[sl]], inc=S_T)
        last_ttk = ttk
        vmx = p.op("vector", lambda e, i=sl:
                   e.reduce_max(sml[:, 55:56], ns.ps_ctx[i][:, 0:8], axis=AX.X, negate=True),
                   waits=[(S_T, ttk)], inc=S_V)
        sxp = p.op("scalar", lambda e, i=sl:
                   e.activation(ctok[:, 0:8], ns.ps_ctx[i][:, 0:8], AF.Exp,
                                bias=sml[:, 55:56], scale=1.0, accum_out=sml[:, 56:57]),
                   waits=[(S_V, vmx)], inc=S_S)
        vr1 = p.op("vector", lambda e: e.reciprocal(sml[:, 57:58], sml[:, 56:57]),
                   waits=[(S_S, sxp)], inc=S_V)
        vsc = p.op("vector", lambda e: e.tensor_scalar(
            ctok[:, 8:16], ctok[:, 0:8], sml[:, 57:58], 8.0, ALU.mult, ALU.mult),
            waits=[(S_V, vr1)], inc=S_V)
        return vsc

    def softmax_ttb(tb, vsc):
        nonlocal v_sT
        sl = tb % 4
        ttb = p.op("tensor", lambda e, i=sl:
                   e.transpose(ns.ps_ctx[i][0:8, 0:128], ctok[:, 8:16], ns.id_f[:]),
                   waits=[(S_V, vsc)], inc=S_T)
        v_sT = p.op("vector", lambda e, i=sl, r=tb * 128:
                    e.tensor_copy(stB[0:8, r:r + 128], ns.ps_ctx[i][0:8, 0:128]),
                    waits=[(S_T, ttb)] + ([ns.stB_free] if tb == 0 else []), inc=S_V)
        tr_evac[sl] = (S_V, v_sT)

    cpt = max(1, NEC // TBL)
    vsc_pend = None
    tb_next = 0
    for c in range(NEC):
        for tc in range(TCH):
            w = [acat_c, (S_V, s_lg)] if (c == 0 and tc == 0) else ()
            t_low = p.op("tensor", lambda e, c=c, tc=tc, st=(c == 0), sp=(c == NEC - 1):
                         e.matmul(psm[:, tc * 512:(tc + 1) * 512],
                                  acat_buf[:, c * 128:(c + 1) * 128],
                                  big[:, c * T + tc * 512:c * T + tc * 512 + 512],
                                  start=st, stop=sp),
                         waits=w, inc=(S_T if (c == NEC - 1 and tc == TCH - 1) else None))
        if c % cpt == cpt - 1 or c == NEC - 1:
            if vsc_pend is not None:
                softmax_ttb(*vsc_pend)
                vsc_pend = None
            if tb_next < TBL:
                vsc_pend = (tb_next, softmax_ttk(tb_next))
                tb_next += 1
    while tb_next < TBL or vsc_pend is not None:
        if vsc_pend is not None:
            softmax_ttb(*vsc_pend)
            vsc_pend = None
        if tb_next < TBL:
            vsc_pend = (tb_next, softmax_ttk(tb_next))
            tb_next += 1
    ns.wb_free[acat_sel] = (S_T, t_low)
    ns.ctx_free = [v for (_s, v) in tr_evac]

    v_scb = p.op("vector", lambda e: e.tensor_copy(ns.qk_raw[0:8, 0:T], stB[0:8, 0:T]),
                 waits=[(S_V, v_sT)], inc=S_V)
    ns.stB_free = (S_V, v_scb)
    lev = []
    for tc in range(TCH):
        w = [(S_T, t_low), (S_T, last_ttk)] if tc == 0 else ()
        le = p.op("vector", lambda e, tc=tc:
                  e.tensor_copy(stA[:, tc * 512:(tc + 1) * 512],
                                psm[:, tc * 512:(tc + 1) * 512]),
                  waits=w, inc=S_V)
        lev.append(le)

    for tc in range(TCH):
        w = [(S_V, v_scb), (S_V, lev[tc])]
        t_ex = p.op("tensor", lambda e, tc=tc:
                    e.matmul(psm[:, tc * 512:(tc + 1) * 512], ns.ee_b[:, 0:128],
                             ns.qk_raw[0:8, tc * 512:(tc + 1) * 512],
                             start=True, stop=True),
                    waits=w, inc=(S_T if tc == TCH - 1 else None))
    for tc in range(TCH):
        v_lo = p.op("vector", lambda e, tc=tc: e.tensor_tensor(
            ns.lows[:, tc * 512:(tc + 1) * 512], psm[:, tc * 512:(tc + 1) * 512],
            stA[:, tc * 512:(tc + 1) * 512], ALU.mult),
            waits=[(S_T, t_ex)] if tc == 0 else (), inc=(S_V if tc == TCH - 1 else None))
    ns.stA_free2 = (S_V, v_lo)
    ns.psm_free = (S_V, v_lo)


def emit_phase2(p, ns):
    T, E, NEC, TCH, NJ, HKV, GQ = ns.T, ns.E, ns.NEC, ns.TCH, ns.NJ, ns.HKV, ns.GQ
    S_T, S_V, S_S, S_G = ns.S_T, ns.S_V, ns.S_S, ns.S_G
    D_OUT = ns.D_OUT
    big, psm = ns.big, ns.ps_main
    qk, qb, kb, vT, t12, pT = ns.qk_raw, ns.q_b, ns.k_b, ns.vT_b, ns.t12, ns.pT
    cosb, sinb = ns.cos_b, ns.sin_b

    vtv = vT.rearrange("p (j c) -> p j c", c=129)[:, :, 128:129]
    p.op("gpsimd", lambda e, o=vtv: e.memset(o, 1.0),
         waits=list(ns.xnT_done2), inc=S_G)

    ns.proj_gate = {}
    qkraw_free = [0, 0]
    qb_free = [0, 0]
    kb_free = 0
    vt_free = 0
    ctok_free = [0, 0, 0, 0]
    ctst_free = 0
    pending = []
    pt_free = [0] * 6

    def proj(wb_, nout, conv_val, extra=(), tcs=None, gates=None):
        if tcs is None:
            tcs = list(range(TCH))
        lastv = 0
        for c in range(NEC):
            for tc in tcs:
                w = []
                if c == 0 and tc == tcs[0]:
                    w = [conv_val, ns.psm_free] + list(extra)
                gate = gates is not None and tc == tcs[-1] and c in gates
                lastv = p.op("tensor", lambda e, c=c, tc=tc, wb=wb_,
                             st=(c == 0), sp=(c == NEC - 1):
                             e.matmul(psm[:, tc * 512:(tc + 1) * 512],
                                      wb[:, c * 128:(c + 1) * 128],
                                      big[:, c * T + tc * 512:c * T + tc * 512 + 512],
                                      start=st, stop=sp),
                             waits=w,
                             inc=(S_T if (gate or (c == NEC - 1 and tc == tcs[-1]))
                                  else None))
                if gate:
                    ns.proj_gate[c] = lastv
        return lastv

    def evac_bf16(dst, tproj, extra=(), tcs=None):
        if tcs is None:
            tcs = list(range(TCH))
        lastv = 0
        for tc in tcs:
            w = [(S_T, tproj)] + list(extra) if tc == tcs[0] else ()
            lastv = p.op("scalar", lambda e, tc=tc, d=dst:
                         e.copy(d[:, tc * 512:(tc + 1) * 512], psm[:, tc * 512:(tc + 1) * 512]),
                         waits=w, inc=(S_S if tc == tcs[-1] else None))
        return lastv

    def rope(dst, war_free, c0=0, c1=None):
        if c1 is None:
            c1 = T
        p.op("vector", lambda e: e.tensor_tensor(
            t12[0:64, c0:c1], qk[64:128, c0:c1], sinb[64:128, c0:c1], ALU.mult))
        p.op("vector", lambda e: e.tensor_tensor(
            t12[64:128, c0:c1], qk[0:64, c0:c1], sinb[0:64, c0:c1], ALU.mult))
        p.op("vector", lambda e, d=dst: e.tensor_tensor(
            d[0:64, c0:c1], qk[0:64, c0:c1], cosb[0:64, c0:c1], ALU.mult),
            waits=[(S_T, war_free)] if war_free else ())
        v4 = p.op("vector", lambda e, d=dst: e.tensor_tensor(
            d[64:128, c0:c1], qk[64:128, c0:c1], cosb[64:128, c0:c1], ALU.mult),
            inc=S_V)
        p.op("vector", lambda e, d=dst: e.tensor_sub(
            d[0:64, c0:c1], d[0:64, c0:c1], t12[0:64, c0:c1]), waits=[(S_V, v4)])
        return p.op("vector", lambda e, d=dst: e.tensor_add(
            d[64:128, c0:c1], d[64:128, c0:c1], t12[64:128, c0:c1]), inc=S_V)

    HB = max(ns.TBL // 2, 1)

    def flush_pending(nmax=None):
        nonlocal ctst_free
        cnt = 0
        while pending and (nmax is None or cnt < nmax):
            tqb, tqB, ssc_val, first_half = pending.pop(0)
            ttc = p.op("tensor", lambda e, i=tqb, s=tqb:
                       e.transpose(ns.ps_ctx[i][:, 0:128], ns.ctok[:, s * 128:(s + 1) * 128],
                                   ns.id_f[:]),
                       waits=[(S_S, ssc_val)], inc=S_T)
            ctok_free[tqb] = ttc
            w = [(S_T, ttc)]
            if tqB % HB == 0 and ctst_free:
                w.append((D_OUT, ctst_free))
            vce = p.op("vector", lambda e, i=tqb, c0=(tqB % HB) * 128:
                       e.tensor_copy(ns.ctst[:, c0:c0 + 128], ns.ps_ctx[i][:, 0:128]),
                       waits=w, inc=S_V)
            ns.ctx_free[tqb] = vce
            if tqB == HB - 1 and ns.TBL > HB:
                ctst_free = p.op("scalar", lambda e, r=ns.cur_hg * 128, hw=HB * 128:
                                 e.dma_start(out=ns.ctxs[r:r + 128, 0:hw],
                                             in_=ns.ctst[:, 0:hw]),
                                 waits=[(S_V, vce)], inc=D_OUT, inck=16)
            cnt += 1

    wk_c = [None] * HKV
    wv_c = [None] * HKV
    wq_c = {}
    wk_buf = [None] * HKV
    wv_buf = [None] * HKV
    wq_buf = {}

    def load_slice(dram, col0, tag):
        sel = ns.wsel
        ns.wsel ^= 1
        cval = load_weight_cols(p, ns, dram[:, col0:col0 + 128], ns.wb[sel], E,
                                ns.wb_free[sel])
        return cval, sel

    def load_slice_part(dram, col0, part, sel):
        r0 = part * (E // 2)
        return load_weight_cols(p, ns, dram[r0:r0 + E // 2, col0:col0 + 128],
                                ns.wb[sel], E // 2, ns.wb_free[sel],
                                c_base=part * (E // 256))

    wk_c[0], s0 = load_slice(ns.wkT, 0, "wk0")
    wk_buf[0] = s0
    wv_c[0], s1 = load_slice(ns.wvT, 0, "wv0")
    wv_buf[0] = s1

    for h in range(HKV):
        tkp = proj(ns.wb[wk_buf[h]], T, wk_c[h])
        ns.wb_free[wk_buf[h]] = (S_T, tkp)
        skv = evac_bf16(qk, tkp, extra=([(S_V, max(qkraw_free))]
                                        if max(qkraw_free) else []))
        ns.psm_free = (S_S, skv)
        wq_c[(h, 0)], wq_buf[(h, 0)] = load_slice(ns.wqT, (h * GQ) * 128, "wq")
        p.waits("vector", [(S_S, skv)] + ([(S_T, kb_free)] if kb_free else []))
        krope = rope(kb, 0)
        qkraw_free = [krope, krope]

        tvp = proj(ns.wb[wv_buf[h]], T, wv_c[h])
        ns.wb_free[wv_buf[h]] = (S_T, tvp)
        svv = 0
        for tc in range(TCH):
            w = [(S_T, tvp)] + ([ns.stA_free2] if tc == 0 else [])
            svv = p.op("scalar", lambda e, tc=tc:
                       e.activation(ns.stA[:, tc * 512:(tc + 1) * 512],
                                    psm[:, tc * 512:(tc + 1) * 512], AF.Copy,
                                    bias=0.0, scale=1.0),
                       waits=w if tc == 0 else (), inc=(S_S if tc == TCH - 1 else None))
        ns.psm_free = (S_S, svv)
        flush_pending()
        last_vtr = 0
        for j in range(NJ):
            sl = j % 4
            w = [(S_V, ns.ctx_free[sl])]
            if j == 0:
                w.append((S_S, svv))
            ttv = p.op("tensor", lambda e, j=j, i=sl:
                       e.transpose(ns.ps_ctx[i][:, 0:128], ns.stA[:, j * 128:(j + 1) * 128],
                                   ns.id_f[:]),
                       waits=w, inc=S_T)
            if j == NJ - 1:
                ns.stA_free2 = (S_T, ttv)
            w2 = [(S_T, ttv)]
            if j == 0 and vt_free:
                w2.append((S_T, vt_free))
            vev = p.op("vector", lambda e, j=j, i=sl:
                       e.tensor_copy(vT[:, j * 129:j * 129 + 128], ns.ps_ctx[i][:, 0:128]),
                       waits=w2, inc=S_V)
            ns.ctx_free[sl] = vev
            last_vtr = vev

        for g in range(GQ):
            hg = h * GQ + g
            ns.cur_hg = hg
            if h == HKV - 1 and g == GQ - 1:
                ns.strip_prev = ctst_free
            hsplit = (TCH + 1) // 2
            h_tcs = [list(range(0, hsplit)), list(range(hsplit, TCH))]
            sqes, vqrs = [], []
            tqp = 0
            final_g = (h == HKV - 1 and g == GQ - 1)
            lasthh = 1 if h_tcs[1] else 0
            NSg = HKV * GQ
            bndg = [round(i * (NSg - 1) / 4) for i in range(5)]
            ns.gate_chunks = [min(bndg[k + 1] - 1, NEC - 1) for k in range(4)]
            for hh, tcs in enumerate(h_tcs):
                if not tcs:
                    sqes.append(sqes[-1])
                    vqrs.append(vqrs[-1])
                    continue
                c0, c1 = tcs[0] * 512, (tcs[-1] + 1) * 512
                gts = set(ns.gate_chunks) if (final_g and hh == lasthh) else None
                tqp = proj(ns.wb[wq_buf[(h, g)]], T, wq_c[(h, g)], tcs=tcs,
                           gates=gts)
                wex = [(S_V, qkraw_free[hh])] if qkraw_free[hh] else []
                sq = evac_bf16(qk, tqp, extra=wex, tcs=tcs)
                p.waits("vector", [(S_S, sq)])
                vq = rope(qb, qb_free[hh], c0, c1)
                sqes.append(sq)
                vqrs.append(vq)
                qkraw_free[hh] = vq
            ns.wb_free[wq_buf[(h, g)]] = (S_T, tqp)
            if h == HKV - 1 and g == GQ - 1:
                ns.last_proj = tqp
            half_free = [sqes[0], sqes[1]]

            allg = []
            for tqc in range(TCH):
                njj = 4 * tqc + 4
                gs = [list(range(j0, min(j0 + 2, njj))) for j0 in range(0, njj, 2)]
                for gi, js in enumerate(gs):
                    allg.append((tqc, js, gi == 0, gi == len(gs) - 1))
            stop_val = [0, 0, 0, 0]
            mvals_all = {}
            sxp_all = {}
            vqr_waited = [False, False]
            LAG = 2
            for n in range(len(allg) + LAG):
                trig0 = min(1, len(allg) - 1)
                trig1 = min(6, len(allg) - 1)
                parts_now = []
                if n == trig0:
                    parts_now.append(0)
                    if trig1 == trig0:
                        parts_now.append(1)
                elif n == trig1:
                    parts_now.append(1)
                for part in parts_now:
                    if g + 1 < GQ:
                        if part == 0:
                            wq_buf[(h, g + 1)] = ns.wsel
                            ns.wsel ^= 1
                        wq_c[(h, g + 1)] = load_slice_part(
                            ns.wqT, (h * GQ + g + 1) * 128, part,
                            wq_buf[(h, g + 1)])
                    elif h + 1 < HKV:
                        if part == 0:
                            wk_buf[h + 1] = ns.wsel
                            ns.wsel ^= 1
                        wk_c[h + 1] = load_slice_part(
                            ns.wkT, (h + 1) * 128, part, wk_buf[h + 1])
                if n == min(12, len(allg)) and g == GQ - 1 and h + 1 < HKV:
                    wv_c[h + 1], wv_buf[h + 1] = \
                        load_slice(ns.wvT, (h + 1) * 128, "wv")
                if n < len(allg):
                    tqc, js, _, is_last = allg[n]
                    hf = n % 2
                    need = 0 if tqc < hsplit else 1
                    last_sc = 0
                    for idx, j in enumerate(js):
                        w = []
                        if idx == 0:
                            w = [(S_S, half_free[hf])]
                            if not vqr_waited[need]:
                                w.append((S_V, vqrs[need]))
                                vqr_waited[need] = True
                        last_sc = p.op("tensor", lambda e, j=j, hf=hf, idx=idx, tqc=tqc:
                                       e.matmul(psm[:, hf * 1024 + idx * 512:
                                                    hf * 1024 + (idx + 1) * 512],
                                                kb[:, j * 128:(j + 1) * 128],
                                                qb[:, tqc * 512:(tqc + 1) * 512],
                                                start=True, stop=True),
                                       waits=w,
                                       inc=(S_T if idx == len(js) - 1 else None))
                    if tqc == hsplit - 1 and is_last:
                        qb_free[0] = last_sc
                    if n == len(allg) - 1:
                        qb_free[1] = last_sc
                        if qb_free[0] == 0:
                            qb_free[0] = last_sc
                        if g == GQ - 1:
                            kb_free = last_sc
                    wex = [(S_T, last_sc)]
                    for idx in range(len(js)):
                        if pt_free[(n % 3) * 2 + idx]:
                            wex.append((S_T, pt_free[(n % 3) * 2 + idx]))
                    nsec = len(js)
                    trim = min((j % 4) * 128 if j // 4 == tqc else 0 for j in js)
                    sec = n % 3
                    sxp = p.op("scalar", lambda e, hf=hf, sc=sec, nn=nsec, t0=trim:
                               e.activation(pT[:, sc * 1024 + t0:sc * 1024 + nn * 512],
                                            psm[:, hf * 1024 + t0:hf * 1024 + nn * 512],
                                            AF.Exp, bias=ns.sml[:, 54:55], scale=ISQ),
                               waits=wex, inc=S_S)
                    half_free[hf] = sxp
                    sxp_all[n] = sxp
                    for idx, j in enumerate(js):
                        if j // 4 == tqc:
                            o = (n % 3) * 1024 + idx * 512 + (j % 4) * 128
                            mvals_all[(n, j)] = p.op(
                                "gpsimd", lambda e, o=o:
                                e.tensor_tensor(pT[:, o:o + 128], pT[:, o:o + 128],
                                                ns.cm_b[:], ALU.mult),
                                waits=[(S_S, sxp)], inc=S_G)
                if n >= LAG:
                    m = n - LAG
                    tqc0, js0, first0, last0 = allg[m]
                    hf0 = m % 3
                    if first0:
                        flush_pending()
                    pv_first = True
                    for idx, j in enumerate(js0):
                        tqb_lo = max(0, j - 4 * tqc0)
                        for tqb in range(tqb_lo, 4):
                            tqB = 4 * tqc0 + tqb
                            w = []
                            if tqb == tqb_lo and ((m, j) in mvals_all or pv_first):
                                w.append((S_G, mvals_all[(m, j)])
                                         if (m, j) in mvals_all
                                         else (S_S, sxp_all[m]))
                            if j == 0 and pv_first:
                                w.append((S_V, max(ns.ctx_free)))
                            pv_first = False
                            do_inc = (j == tqB) or (tqb == 3)
                            val = p.op("tensor",
                                       lambda e, j=j, hf=hf0, idx=idx, tqb=tqb, tqB=tqB:
                                       e.matmul(ns.ps_ctx[tqb][:, 0:129],
                                                pT[:, hf * 1024 + idx * 512 + tqb * 128:
                                                   hf * 1024 + idx * 512 + tqb * 128 + 128],
                                                vT[:, j * 129:(j + 1) * 129],
                                                start=(j == 0), stop=(j == tqB)),
                                       waits=w, inc=(S_T if do_inc else None))
                            if j == tqB:
                                stop_val[tqb] = val
                            if tqb == 3:
                                pt_free[hf0 * 2 + idx] = val
                            if g == GQ - 1 and j == NJ - 1:
                                vt_free = val
                    if last0:
                        for tqb in range(4):
                            tqB = 4 * tqc0 + tqb
                            vrc = p.op("vector", lambda e, i=tqb, c=tqB:
                                       e.reciprocal(ns.rsb[:, c:c + 1],
                                                    ns.ps_ctx[i][:, 128:129]),
                                       waits=[(S_T, stop_val[tqb])], inc=S_V)
                            w2 = [(S_V, vrc)]
                            if ctok_free[tqb]:
                                w2.append((S_T, ctok_free[tqb]))
                            ssc = p.op("scalar", lambda e, i=tqb, s=tqb, c=tqB:
                                       e.activation(ns.ctok[:, s * 128:(s + 1) * 128],
                                                    ns.ps_ctx[i][:, 0:128], AF.Identity,
                                                    bias=0.0, scale=ns.rsb[:, c:c + 1]),
                                       waits=w2, inc=S_S)
                            pending.append((tqb, tqB, ssc, None))
            flush_pending()
            h0 = HB * 128 if ns.TBL > HB else 0
            dst = p.op("scalar", lambda e, r=hg * 128, c0=h0:
                       e.dma_start(out=ns.ctxs[r:r + 128, c0:T],
                                   in_=ns.ctst[:, 0:T - h0]),
                       waits=[(S_V, max(ns.ctx_free))], inc=D_OUT, inck=16)
            ctst_free = dst
            ns.psm_free = (S_S, max(half_free))
    NS = HKV * GQ
    bnd = [round(i * (NS - 1) / 4) for i in range(5)]
    ns.big_bnd = bnd
    ns.big_piece = []
    psem = [ns.D_B2[0], ns.D_B2[1], ns.D_B2[2], ns.D_B2[3]]
    for k in range(4):
        a, b = bnd[k], bnd[k + 1]
        if b <= a:
            ns.big_piece.append(None)
            continue
        csrc = ns.ctxs[a * 128:b * 128, :].rearrange("(c p) t -> p c t", p=128)
        cdst = big.rearrange("p (c t) -> p c t", t=T)[:, a:b, :]
        w = [(ns.D_OUT, ns.strip_prev, True),
             (S_T, ns.proj_gate[ns.gate_chunks[k]])]
        v = p.op("sync", lambda e, o=cdst, i=csrc: e.dma_start(out=o, in_=i),
                 waits=w, inc=psem[k], inck=16)
        ns.big_piece.append((psem[k], v))
    ns.wo_pre = {}
    for eb in range(2):
        sel = ns.wsel
        ns.wsel ^= 1
        cval = load_weight_cols(p, ns, ns.woT[:, eb * 128:(eb + 1) * 128],
                                ns.wb[sel], E, ns.wb_free[sel])
        ns.wo_pre[eb] = (cval, sel)
    ns.ctst_dma_final = ctst_free


def emit_phase3(p, ns):
    T, E, NEC, TCH = ns.T, ns.E, ns.NEC, ns.TCH
    S_T, S_V, S_S, S_G = ns.S_T, ns.S_V, ns.S_S, ns.S_G
    D_OUT, D_BIG, D_C = ns.D_OUT, ns.D_BIG, ns.D_C
    big, psB, psW = ns.big, ns.ps_B, ns.ps_W

    NS = E // 128
    p.waits("sync", [(D_OUT, ns.ctst_dma_final)])
    d_ctx = p.op("sync", lambda e, o=big[:, (NS - 1) * T:NS * T],
                 i=ns.ctxs[(NS - 1) * 128:NS * 128, :]:
                 e.dma_start(out=o, in_=i), inc=D_BIG, inck=16)

    def big_gate(c):
        if c >= NS - 1:
            return (D_BIG, d_ctx)
        for k in range(4):
            if c < ns.big_bnd[k + 1]:
                return ns.big_piece[k] or (D_BIG, d_ctx)
        return (D_BIG, d_ctx)
    gate_cs = sorted(set([0] + ns.big_bnd[1:4] + [NS - 1]))

    v_bc = 0
    for h0 in range(0, E, 2048):
        wdt = min(2048, E - h0)
        d = p.op("sync", lambda e, h0=h0, w=wdt:
                 e.dma_start(out=ns.stA[:, 0:w], in_=ns.bcat[:, h0:h0 + w]),
                 waits=[(S_V, v_bc)] if v_bc else [ns.stA_free2], inc=D_C, inck=16)
        v_bc = p.op("vector", lambda e, h0=h0, w=wdt:
                    e.tensor_copy(ns.bc_b[:, h0:h0 + w], ns.stA[:, 0:w]),
                    waits=[(D_C, d)], inc=S_V)
        ns.stA_free3 = (S_V, v_bc)

    wo_c = {eb: cv for eb, (cv, _s) in ns.wo_pre.items()}
    wo_buf = {eb: s for eb, (_cv, s) in ns.wo_pre.items()}

    def load_wo(eb):
        sel = ns.wsel
        ns.wsel ^= 1
        wo_c[eb] = load_weight_cols(p, ns, ns.woT[:, eb * 128:(eb + 1) * 128],
                                    ns.wb[sel], E, ns.wb_free[sel], alt=True)
        wo_buf[eb] = sel
    xT_free = [0, 0]
    psW_free = 0
    s_g_prev = 0
    d_out = 0
    ps2 = [psW, psB]
    for eb in range(NEC):
        wo_ps = ps2[eb % 2]
        b_ps = ps2[1 - eb % 2]
        xsel = eb % 2
        D_X = ns.D_XA if xsel == 0 else ns.D_XB
        d_x = p.op("sync", lambda e, r=eb * 128, s=xsel:
                   e.dma_start(out=ns.xTb[:, s * T:s * T + T], in_=ns.xT[r:r + 128, :]),
                   waits=[(S_G, xT_free[xsel])] if xT_free[xsel] else (),
                   inc=D_X, inck=16)
        t_B = 0
        for tc in range(TCH):
            w = []
            if tc == 0 and eb == 0:
                w.append((S_V, v_bc))
            if psW_free:
                w.append((S_V, psW_free[tc]))
            t_B = p.op("tensor", lambda e, eb=eb, tc=tc, ps=b_ps:
                       e.matmul(ps[:, tc * 512:(tc + 1) * 512],
                                ns.bc_b[:, eb * 128:(eb + 1) * 128],
                                ns.lows[:, tc * 512:(tc + 1) * 512],
                                start=True, stop=True),
                       waits=w, inc=(S_T if tc == TCH - 1 else None))
        s_g = 0
        for tc in range(TCH):
            w = [(S_T, t_B)]
            if tc == 0 and d_out:
                w.append((D_OUT, d_out))
            s_g = p.op("scalar", lambda e, tc=tc, ps=b_ps:
                       e.activation(ns.outb[:, tc * 512:(tc + 1) * 512],
                                    ps[:, tc * 512:(tc + 1) * 512], AF.Silu,
                                    bias=0.0, scale=2.0),
                       waits=w if tc == 0 else (), inc=(S_S if tc == TCH - 1 else None))
        t_W = 0
        wo_strip = []
        for tc in range(TCH):
            for c in range(NEC):
                w = []
                if c == 0 and tc == 0:
                    w = [wo_c[eb]]
                    if s_g_prev:
                        w.append((S_S, s_g_prev))
                if eb == 0 and tc == 0 and c in gate_cs:
                    w.append(big_gate(c))
                t_W = p.op("tensor", lambda e, c=c, tc=tc, wb=ns.wb[wo_buf[eb]],
                           ps=wo_ps, st=(c == 0), sp=(c == NEC - 1):
                           e.matmul(ps[:, tc * 512:(tc + 1) * 512],
                                    wb[:, c * 128:(c + 1) * 128],
                                    big[:, c * T + tc * 512:c * T + tc * 512 + 512],
                                    start=st, stop=sp),
                           waits=w,
                           inc=(S_T if c == NEC - 1 else None))
            wo_strip.append(t_W)
        ns.wb_free[wo_buf[eb]] = (S_T, t_W)
        v_strips = []
        v_cb = 0
        for tc in range(TCH):
            w = [(S_T, wo_strip[tc])]
            if tc == 0:
                w.append((S_S, s_g))
            v_cb = p.op("vector", lambda e, tc=tc, ps=wo_ps:
                        e.tensor_tensor(ns.outb[:, tc * 512:(tc + 1) * 512],
                                        ps[:, tc * 512:(tc + 1) * 512],
                                        ns.outb[:, tc * 512:(tc + 1) * 512], ALU.mult),
                        waits=w, inc=S_V)
            v_strips.append(v_cb)
        psW_free = v_strips
        s_g_prev = s_g
        if eb + 2 < NEC:
            load_wo(eb + 2)
        g_cb = 0
        for tc in range(TCH):
            w = [(S_V, v_cb), (D_X, d_x)] if tc == 0 else ()
            g_cb = p.op("gpsimd", lambda e, tc=tc, s=xsel:
                        e.tensor_add(ns.outb[:, tc * 512:(tc + 1) * 512],
                                     ns.outb[:, tc * 512:(tc + 1) * 512],
                                     ns.xTb[:, s * T + tc * 512:s * T + (tc + 1) * 512]),
                        waits=w, inc=(S_G if tc == TCH - 1 else None))
        xT_free[xsel] = g_cb
        d_out = p.op("scalar", lambda e, r=eb * 128:
                     e.dma_start(out=ns.outT[r:r + 128, :], in_=ns.outb[:, 0:T]),
                     waits=[(S_G, g_cb)], inc=D_OUT, inck=16)
    p.waits("scalar", [(D_OUT, d_out)])



def _rope_tables(T):
    DH, theta, factor, hi_f, lo_f, orig = 128, 500000.0, 32.0, 4.0, 1.0, 8192
    i = np.arange(0, DH, 2, dtype=np.float64)
    inv = 1.0 / theta ** (i / DH)
    wav = 2.0 * math.pi / inv
    cut_lo, cut_hi = orig / hi_f, orig / lo_f
    inv = np.where(wav > cut_hi, inv / factor, inv)
    smooth = (orig / wav - lo_f) / (hi_f - lo_f)
    smoothed = (1.0 - smooth) * inv / factor + smooth * inv
    is_med = ~(wav < cut_hi) & ~(wav > cut_lo)
    inv = np.where(is_med, smoothed, inv)
    pos = np.arange(T, dtype=np.float64)
    fr = pos[:, None] * inv[None, :]
    emb = np.concatenate([fr, fr], axis=1).T
    out = []
    for tab in (np.cos(emb), np.sin(emb)):
        hi = tab.astype(np.float32)
        lo = (tab - hi.astype(np.float64)).astype(np.float32)
        out += [np.ascontiguousarray(hi), np.ascontiguousarray(lo)]
    return out


def make_inputs(cfg, x, Wq, Wk, Wv, Wo, A, B, Wr, br):
    T, E = cfg["T"], cfg["E"]
    f = np.float32
    ch, cl, sh, sl = _rope_tables(T)
    shared = dict(
        wqT=np.ascontiguousarray(np.asarray(Wq, f).T),
        wkT=np.ascontiguousarray(np.asarray(Wk, f).T),
        wvT=np.ascontiguousarray(np.asarray(Wv, f).T),
        woT=np.ascontiguousarray(np.asarray(Wo, f).T),
        acatT=np.ascontiguousarray(np.asarray(A, f).reshape(128, E).T),
        bcat=np.ascontiguousarray(np.asarray(B, f).transpose(0, 2, 1).reshape(128, E)),
        wrT=np.ascontiguousarray(np.asarray(Wr, f).T),
        br=np.ascontiguousarray(np.asarray(br, f).reshape(8, 1)),
        cos_hi=ch, cos_lo=cl, sin_hi=sh, sin_lo=sl,
        identf=np.eye(128, dtype=f),
        cmaskf=np.ascontiguousarray(
            (np.arange(128)[:, None] <= np.arange(128)[None, :]).astype(f)),
        eexpf=np.ascontiguousarray(
            (np.arange(128)[None, :] // 16 == np.arange(8)[:, None]).astype(f)),
    )
    in_maps = []
    for b in range(x.shape[0]):
        m = dict(shared)
        m["x_tok"] = np.ascontiguousarray(np.asarray(x[b], f))
        m["xT"] = np.ascontiguousarray(np.asarray(x[b], f).T)
        in_maps.append(m)
    return in_maps


def kernel(L0_attn_x, L0_attn_Wq, L0_attn_Wk, L0_attn_Wv, L0_attn_Wo,
           L0_ffn_A, L0_ffn_B, L0_ffn_Wr, L0_ffn_br):
    cfg = FULL
    x = np.asarray(L0_attn_x, np.float32)
    in_maps = make_inputs(cfg, x, L0_attn_Wq, L0_attn_Wk, L0_attn_Wv, L0_attn_Wo,
                          L0_ffn_A, L0_ffn_B, L0_ffn_Wr, L0_ffn_br)
    nc = build_nc(cfg)
    res = run_bass_kernel_spmd(nc, in_maps, core_ids=list(range(len(in_maps))))
    out = np.stack([np.asarray(r["outT"], np.float32).T for r in res.results])
    return np.ascontiguousarray(out)


# version 28
# speedup vs baseline: 1.1073x; 1.0050x over previous
import math
import numpy as np
from types import SimpleNamespace
from contextlib import ExitStack

import concourse.bass as bass
import concourse.mybir as mybir
from concourse.bass_utils import run_bass_kernel_spmd

F32 = mybir.dt.float32
BF16 = mybir.dt.bfloat16
AF = mybir.ActivationFunctionType
ALU = mybir.AluOpType
AX = mybir.AxisListType

CSHIFT = 12.0
ISQ = 1.0 / math.sqrt(128.0)
LN_EPS = 1e-5

FULL = dict(T=2048, E=4096, HKV=8, GQ=4)


class Sem:
    def __init__(self, name, dma=False):
        self.name, self.n, self.h, self.dma = name, 0, None, dma

    def bump(self, k=1):
        self.n += k
        return self.n


class Prog:
    ENGS = ("sync", "tensor", "vector", "scalar", "gpsimd")

    def __init__(self):
        self.st = {k: [] for k in self.ENGS}
        self.ninstr = 0
        self.hiwater = {}

    def _emit_wait(self, eng, sem, val):
        self.st[eng].append(lambda e, s=sem, v=val: e.wait_ge(s.h, v))
        self.ninstr += 1

    def waits(self, eng, waits):
        for wspec in waits:
            if len(wspec) == 3:
                sem, val, exact = wspec
                if val > 0:
                    self._emit_wait(eng, sem, val)
                continue
            sem, val = wspec
            if sem.dma:
                val = sem.n if val > 0 else 0
            if val > 0:
                self._emit_wait(eng, sem, val)

    def op(self, eng, fn, waits=(), inc=None, inck=1):
        self.waits(eng, waits)
        self.ninstr += 1
        if inc is not None:
            val = inc.bump(inck)
            self.st[eng].append(lambda e, f=fn, s=inc, k=inck: f(e).then_inc(s.h, k))
            return val
        self.st[eng].append(fn)
        return None

    def replay(self, block):
        for name in self.ENGS:
            ops = self.st[name]

            def mk(ops):
                def run(eng):
                    for fn in ops:
                        fn(eng)
                return run
            getattr(block, name)(mk(ops))
        self.st = {k: [] for k in self.ENGS}


def build_nc(cfg):
    T, E, HKV, GQ = cfg["T"], cfg["E"], cfg["HKV"], cfg["GQ"]
    NEC, TBL, TCH, NJ = E // 128, T // 128, T // 512, T // 128
    assert GQ * HKV * 128 == E and T % 512 == 0 and E % 256 == 0

    nc = bass.Bass("TRN2", target_bir_lowering=False)
    dtt = nc.dram_tensor
    ns = SimpleNamespace(cfg=cfg, NEC=NEC, TBL=TBL, TCH=TCH, NJ=NJ, T=T, E=E,
                         HKV=HKV, GQ=GQ)
    ns.x_tok = dtt("x_tok", [T, E], F32, kind="ExternalInput").ap()
    ns.xT = dtt("xT", [E, T], F32, kind="ExternalInput").ap()
    ns.wqT = dtt("wqT", [E, E], F32, kind="ExternalInput").ap()
    ns.wkT = dtt("wkT", [E, HKV * 128], F32, kind="ExternalInput").ap()
    ns.wvT = dtt("wvT", [E, HKV * 128], F32, kind="ExternalInput").ap()
    ns.woT = dtt("woT", [E, E], F32, kind="ExternalInput").ap()
    ns.acatT = dtt("acatT", [E, 128], F32, kind="ExternalInput").ap()
    ns.bcat = dtt("bcat", [128, E], F32, kind="ExternalInput").ap()
    ns.wrT = dtt("wrT", [E, 8], F32, kind="ExternalInput").ap()
    ns.brh = dtt("br", [8, 1], F32, kind="ExternalInput").ap()
    ns.cos_hi = dtt("cos_hi", [128, T], F32, kind="ExternalInput").ap()
    ns.cos_lo = dtt("cos_lo", [128, T], F32, kind="ExternalInput").ap()
    ns.sin_hi = dtt("sin_hi", [128, T], F32, kind="ExternalInput").ap()
    ns.sin_lo = dtt("sin_lo", [128, T], F32, kind="ExternalInput").ap()
    ns.identf = dtt("identf", [128, 128], F32, kind="ExternalInput").ap()
    ns.cmaskf = dtt("cmaskf", [128, 128], F32, kind="ExternalInput").ap()
    ns.eexpf = dtt("eexpf", [8, 128], F32, kind="ExternalInput").ap()
    ns.ctxs = dtt("ctxs", [E, T], BF16, kind="Internal").ap()
    ns.outT = dtt("outT", [E, T], F32, kind="ExternalOutput").ap()

    p = Prog()
    ns.S_T, ns.S_V, ns.S_S, ns.S_G = Sem("sT"), Sem("sV"), Sem("sS"), Sem("sG")
    ns.D_W = Sem("dW", dma=True)
    ns.D_W2 = Sem("dW2", dma=True)
    ns.D_XA = Sem("dXA", dma=True)
    ns.D_XB = Sem("dXB", dma=True)
    ns.D_XC = Sem("dXC", dma=True)
    ns.D_XD = Sem("dXD", dma=True)
    ns.D_C = Sem("dC", dma=True)
    ns.D_BIG = Sem("dBIG", dma=True)
    ns.D_B2 = [Sem("dB2_%d" % i, dma=True) for i in range(4)]
    ns.D_X3 = [Sem("dX3_%d" % i, dma=True) for i in range(4)]
    ns.D_GP = [Sem("dGP_%d" % i, dma=True) for i in range(2)]
    ns.D_OUT = Sem("dOUT", dma=True)
    sems = [ns.S_T, ns.S_V, ns.S_S, ns.S_G, ns.D_W, ns.D_W2, ns.D_XA, ns.D_XB,
            ns.D_XC, ns.D_XD, ns.D_C, ns.D_BIG, ns.D_OUT] + ns.D_B2 + ns.D_X3 \
        + ns.D_GP

    with ExitStack() as ctx:
        sb = nc.sbuf_tensor
        ns.big = ctx.enter_context(sb("big", [128, NEC * T], BF16))
        ns.stA = ctx.enter_context(sb("stA", [128, 2048], F32))
        ns.stB = ctx.enter_context(sb("stB", [128, 2048], F32))
        ns.wb = [ctx.enter_context(sb("wb0", [128, E], BF16)),
                 ctx.enter_context(sb("wb1", [128, E], BF16))]
        ns.lows = ctx.enter_context(sb("lows", [128, T], BF16))
        ns.id_f = ctx.enter_context(sb("id_f", [128, 128], F32))
        ns.cm_b = ctx.enter_context(sb("cm_b", [128, 128], BF16))
        ns.ee_b = ctx.enter_context(sb("ee_b", [8, 128], BF16))
        ns.br_s = ctx.enter_context(sb("br_s", [8, 1], F32))
        ns.sml = ctx.enter_context(sb("sml", [128, 216], F32))
        ns.rsb = ctx.enter_context(sb("rsb", [128, 16], F32))
        for s in sems:
            s.h = ctx.enter_context(nc.semaphore(s.name))

        ns.stA_free = (ns.S_T, 0)
        ns.stA_free2 = (ns.S_V, 0)
        ns.stB_free = (ns.S_V, 0)
        ns.wb_free = [(ns.S_T, 0), (ns.S_T, 0)]
        ns._alt_flip = [0]
        ns.stA_free3 = (ns.S_V, 0)
        ns.wsel = 0
        ns.ctx_free = [0, 0, 0, 0]
        ns.xnT_done = 0

        with ExitStack() as cA:
            ns.cos_b = cA.enter_context(sb("cos_b", [128, T], BF16))
            ns.sin_b = cA.enter_context(sb("sin_b", [128, T], BF16))
            ns.ps_main = cA.enter_context(nc.psum_tensor("ps_main", [128, 2048], F32))
            ns.ps_ctx = [cA.enter_context(nc.psum_tensor("ps_ctx%d" % i, [128, 129], F32))
                         for i in range(4)]
            with ExitStack() as c0:
                ns.stC = c0.enter_context(sb("stC", [128, 2048], F32))
                ns.stD = c0.enter_context(sb("stD", [128, 2048], F32))
                ns.stE = c0.enter_context(sb("stE", [128, 2048], F32))
                ns.stF = c0.enter_context(sb("stF", [128, 2048], F32))
                emit_phase0(p, ns)
            ns.vT_b = cA.enter_context(sb("vT_b", [128, NJ * 129], BF16))
            ns.qk_raw = cA.enter_context(sb("qk_raw", [128, T], BF16))
            ns.q_b = cA.enter_context(sb("q_b", [128, T], BF16))
            ns.k_b = cA.enter_context(sb("k_b", [128, T], BF16))
            ns.t12 = cA.enter_context(sb("t12", [128, max(T, NEC * 8)], BF16))
            ns.pT = cA.enter_context(sb("pT", [128, 3072], BF16))
            ns.ctst = cA.enter_context(sb("ctst", [128, max(T // 2, 512)], BF16))
            ns.ctok = cA.enter_context(sb("ctok", [128, 512], F32))
            blkA = cA.enter_context(nc.Block("A"))
            emit_phase1(p, ns)
            emit_phase2(p, ns)
            p.replay(blkA)

        with ExitStack() as cB:
            ns.xTb = cB.enter_context(sb("xTb", [128, 2 * T], F32))
            ns.outb = cB.enter_context(sb("outb", [128, T], F32))
            ns.bc_b = cB.enter_context(sb("bc_b", [128, E], BF16))
            ns.ps_B = cB.enter_context(nc.psum_tensor("ps_B", [128, 2048], F32))
            ns.ps_W = cB.enter_context(nc.psum_tensor("ps_W", [128, 2048], F32))
            blkB = cB.enter_context(nc.Block("B"))
            emit_phase3(p, ns)
            p.replay(blkB)

    nc._ninstr = p.ninstr
    return nc


def load_weight_cols(p, ns, dram_slice, wbuf, n_rows, wb_free, alt=False,
                     conv_eng="vector", c_base=0):
    done, out, first = 0, 0, True
    while done < n_rows:
        rows = min(2048, n_rows - done)
        nch = rows // 128
        stg, freekey, dsem = (ns.stB, "stB_free", ns.D_W)
        if alt and ns._alt_flip[0]:
            stg, freekey, dsem = (ns.stA, "stA_free3", ns.D_W2)
        if alt:
            ns._alt_flip[0] ^= 1
        src = dram_slice[done:done + rows, :].rearrange("(c p) m -> p c m", p=128)
        dst = stg.rearrange("p (c m) -> p c m", m=128)[:, 0:nch, :]
        dval = p.op("sync", lambda e, o=dst, i=src: e.dma_start(out=o, in_=i),
                    waits=[getattr(ns, freekey)], inc=dsem, inck=16)
        c0 = c_base + done // 128
        w = [(dsem, dval)]
        if first:
            w.append(wb_free)
            first = False
        csem = ns.S_G if conv_eng == "gpsimd" else ns.S_V
        out = p.op(conv_eng,
                   lambda e, o=wbuf[:, c0 * 128:(c0 + nch) * 128], i=stg[:, 0:rows]:
                   e.tensor_copy(o, i), waits=w, inc=csem)
        setattr(ns, freekey, (csem, out))
        done += rows
    return (csem, out)


def emit_phase0(p, ns):
    T, E, NEC, TBL = ns.T, ns.E, ns.NEC, ns.TBL
    S_T, S_V, S_S, S_G = ns.S_T, ns.S_V, ns.S_S, ns.S_G
    D_C, D_XA, D_XB = ns.D_C, ns.D_XA, ns.D_XB
    stA, stB, big, sml = ns.stA, ns.stB, ns.big, ns.sml

    p.op("vector", lambda e: e.memset(sml[:, 53:54], LN_EPS))
    p.op("vector", lambda e: e.memset(sml[:, 54:55], -CSHIFT), inc=S_V)
    cstate = {}

    def emit_const_dmas():
        cstate["id"] = p.op("sync", lambda e:
                            e.dma_start(out=ns.id_f[:], in_=ns.identf[:, :]),
                            inc=D_C, inck=16)
        cstate["cm"] = p.op("sync", lambda e:
                            e.dma_start(out=stA[:, 0:128], in_=ns.cmaskf[:, :]),
                            inc=D_C, inck=16)
        cstate["ee"] = p.op("sync", lambda e:
                            e.dma_start(out=stA[0:8, 128:256], in_=ns.eexpf[:, :]),
                            inc=D_C, inck=16)
        p.op("sync", lambda e: e.dma_start(out=ns.br_s[:], in_=ns.brh[:, :]),
             inc=D_C, inck=16)

    def emit_const_convs():
        p.op("vector", lambda e: e.tensor_copy(ns.cm_b[:], stA[:, 0:128]),
             waits=[(D_C, cstate["cm"])], inc=S_V)
        v_ee = p.op("vector", lambda e: e.tensor_copy(ns.ee_b[:], stA[0:8, 128:256]),
                    waits=[(D_C, cstate["ee"])], inc=S_V)
        ns.stA_free2 = (S_V, v_ee)

    tstate = {}

    def tables_dma(hi, lo):
        dh = p.op("sync", lambda e, i=hi: e.dma_start(out=stB[:, 0:T], in_=i[:, :]),
                  waits=[ns.stB_free], inc=ns.D_W, inck=16)
        dl = p.op("sync", lambda e, i=lo: e.dma_start(out=stA[:, 0:T], in_=i[:, :]),
                  waits=[ns.stA_free2], inc=ns.D_W2, inck=16)
        return dh, dl

    def tables_add():
        dh, dl = tstate["dma"]
        vad = p.op("vector", lambda e: e.tensor_add(stB[:, 0:T], stB[:, 0:T], stA[:, 0:T]),
                   waits=[(ns.D_W, dh), (ns.D_W2, dl)], inc=S_V)
        ns.stA_free2 = (S_V, vad)
        tstate["add"] = vad

    def tables_copy(dstb):
        vc = p.op("vector", lambda e, o=dstb: e.tensor_copy(o[:, 0:T], stB[:, 0:T]),
                  waits=[(S_V, tstate["add"])], inc=S_V)
        ns.stB_free = (S_V, vc)

    def tp0():
        tables_add()

    def tp1():
        tables_copy(ns.cos_b)
        tstate["dma"] = tables_dma(ns.sin_hi, ns.sin_lo)

    def tp2():
        tables_add()

    def tp3():
        tables_copy(ns.sin_b)
        q_free[0][:] = [ns.stA_free2, ns.stA_free2, ns.stB_free, ns.stB_free]
    tparts = [tp0, tp1, tp2, tp3]

    QL = E // 4
    nbc = max(1, QL // 512)
    bnw = min(512, QL)
    ncq = NEC // 4
    DXQ = [[ns.D_XA, ns.D_XB, ns.D_XC, ns.D_XD], ns.D_B2, ns.D_X3]
    qbuf = [[(stA, 0), (stA, QL), (stB, 0), (stB, QL)],
            [(ns.stC, 0), (ns.stC, QL), (ns.stD, 0), (ns.stD, QL)],
            [(ns.stE, 0), (ns.stE, QL), (ns.stF, 0), (ns.stF, QL)]]
    q_free = [[ns.stA_free2, ns.stA_free2, ns.stB_free, ns.stB_free],
              [(S_V, 0)] * 4, [(S_V, 0)] * 4]
    SSET = [(0, 48, 50, 51, 52), (96, 144, 146, 147, 148),
            (160, 208, 210, 211, 212)]
    NSL = 8
    tr_evac = [(S_V, 0)] * NSL

    def slot_ap(sl):
        if sl < 4:
            return ns.ps_main[:, sl * 512:sl * 512 + 128]
        return ns.ps_ctx[sl - 4][:, 0:128]

    def pari(i):
        return (i % 2) + 1 if i < 6 else i % 3

    def stage_dma(tb):
        pp = pari(tb)
        r = tb * 128
        dq = []
        for qq in range(4):
            buf, off = qbuf[pp][qq]
            w = [q_free[pp][qq]]
            dq.append(p.op("sync", lambda e, r=r, b=buf, o=off, qq=qq:
                           e.dma_start(out=b[:, o:o + QL],
                                       in_=ns.x_tok[r:r + 128, qq * QL:(qq + 1) * QL]),
                           waits=w, inc=DXQ[pp][qq], inck=16))
        return dq

    def stage_stats_bn(tb, dq):
        pp = pari(tb)
        sb0, agg, std, rstd, nmr = SSET[pp]
        vbs = 0
        for qq in range(4):
            buf, off = qbuf[pp][qq]
            for q in range(nbc):
                k6 = sb0 + (qq * nbc + q) * 6
                vbs = p.op("vector", lambda e, b=buf, o=off, q=q, k6=k6:
                           e.bn_stats(sml[:, k6:k6 + 6],
                                      b[:, o + q * bnw:o + (q + 1) * bnw]),
                           waits=[(DXQ[pp][qq], dq[qq])] if q == 0 else (),
                           inc=(S_V if (qq == 3 and q == nbc - 1) else None))
        v3 = p.op("vector", lambda e, a=agg, b0=sb0, n=4 * nbc * 6:
                  e.bn_aggr(sml[:, a:a + 2], sml[:, b0:b0 + n]),
                  waits=[(S_V, vbs)], inc=S_V)
        return v3

    def stage_stats_chain(tb, v3):
        pp = pari(tb)
        _sb0, agg, std, rstd, nmr = SSET[pp]
        s1 = p.op("scalar", lambda e, a=agg, s=std:
                  e.activation(sml[:, s:s + 1], sml[:, a + 1:a + 2],
                               AF.Sqrt, bias=sml[:, 53:54], scale=1.0),
                  waits=[(S_V, v3)], inc=S_S)
        vrc0 = p.op("vector", lambda e, s=std, rs=rstd:
                    e.reciprocal(sml[:, rs:rs + 1], sml[:, s:s + 1]),
                    waits=[(S_S, s1)], inc=S_V)
        v5 = p.op("vector", lambda e, a=agg, rs=rstd, nm=nmr:
                  e.scalar_tensor_tensor(
                      sml[:, nm:nm + 1], sml[:, a:a + 1], -1.0,
                      sml[:, rs:rs + 1], ALU.mult, ALU.mult),
                  waits=[(S_V, vrc0)], inc=S_V)
        return v5

    def stage_affine(tb, v5):
        pp = pari(tb)
        _sb0, _agg, _std, rstd, nmr = SSET[pp]
        sqs = []
        for qq in range(4):
            buf, off = qbuf[pp][qq]
            eng, sem = ("vector", S_V) if qq % 2 == 0 else ("gpsimd", S_G)
            sq = p.op(eng, lambda e, b=buf, o=off, rs=rstd, nm=nmr:
                      e.tensor_scalar(b[:, o:o + QL], b[:, o:o + QL],
                                      sml[:, rs:rs + 1], sml[:, nm:nm + 1],
                                      ALU.mult, ALU.add),
                      waits=[(S_V, v5)] if qq < 2 else (), inc=sem)
            sqs.append((sem, sq))
        return sqs

    def stage_transpose(tb, sqs):
        pp = pari(tb)
        r = tb * 128
        for qq in range(4):
            buf, off = qbuf[pp][qq]
            for cc in range(ncq):
                c = qq * ncq + cc
                sl = c % NSL
                w_tr = [sqs[qq], tr_evac[sl]]
                if tb == 0 and c == 0:
                    w_tr.append((D_C, cstate["id"]))
                tv = p.op("tensor", lambda e, b=buf, c0=off + cc * 128, sl=sl:
                          e.transpose(slot_ap(sl), b[:, c0:c0 + 128],
                                      ns.id_f[:]),
                          waits=w_tr, inc=S_T)
                ev = p.op("scalar", lambda e, sl=sl, c=c, r=r:
                          e.copy(big[:, c * T + r:c * T + r + 128],
                                 slot_ap(sl)),
                          waits=[(S_T, tv)], inc=S_S)
                tr_evac[sl] = (S_S, ev)
                if cc == ncq - 1:
                    q_free[pp][qq] = (S_T, tv)

    dq = {0: stage_dma(0)}
    if TBL > 1:
        dq[1] = stage_dma(1)
    emit_const_dmas()
    v5s = {0: stage_stats_chain(0, stage_stats_bn(0, dq[0]))}
    sqs_pre = {0: stage_affine(0, v5s[0])}
    if TBL > 1:
        v5s[1] = stage_stats_chain(1, stage_stats_bn(1, dq[1]))
        sqs_pre[1] = stage_affine(1, v5s[1])
    emit_const_convs()
    tstate["dma"] = tables_dma(ns.cos_hi, ns.cos_lo)
    for tb in range(TBL):
        sqs = sqs_pre[tb] if tb in sqs_pre else stage_affine(tb, v5s[tb])
        stage_transpose(tb, sqs)
        if tb + 2 < TBL:
            dq[tb + 2] = stage_dma(tb + 2)
            v5s[tb + 2] = stage_stats_chain(tb + 2,
                                            stage_stats_bn(tb + 2, dq[tb + 2]))
        if tb < 4:
            tparts[tb]()
        ns.xnT_done2 = [(S_V, S_V.n), (S_S, S_S.n), (S_G, S_G.n)]
    ns.ctx_free0 = tr_evac[4:8]
    ns.stA_free2 = max(q_free[0][0], q_free[0][1], key=lambda t: t[1])
    ns.stB_free = max(q_free[0][2], q_free[0][3], key=lambda t: t[1])


def emit_phase1(p, ns):
    T, E, NEC, TBL, TCH = ns.T, ns.E, ns.NEC, ns.TBL, ns.TCH
    S_T, S_V, S_S = ns.S_T, ns.S_V, ns.S_S
    D_C = ns.D_C
    stA, stB, big, sml = ns.stA, ns.stB, ns.big, ns.sml
    psm, ctok = ns.ps_main, ns.ctok

    src = ns.wrT.rearrange("(c p) m -> p c m", p=128)
    dst = stA.rearrange("p (c m) -> p c m", m=8)[:, 0:NEC, :]
    d_wr = p.op("sync", lambda e, o=dst, i=src: e.dma_start(out=o, in_=i),
                waits=[ns.stA_free2], inc=D_C, inck=16)
    v_wr = p.op("vector", lambda e, n=NEC * 8: e.tensor_copy(ns.t12[:, 0:n], stA[:, 0:n]),
                waits=[(D_C, d_wr)] + list(ns.xnT_done2), inc=S_V)

    acat_c = load_weight_cols(p, ns, ns.acatT[:, 0:128], ns.wb[ns.wsel], E,
                              ns.wb_free[ns.wsel])
    acat_buf = ns.wb[ns.wsel]
    acat_sel = ns.wsel
    ns.wsel ^= 1

    last = None
    for c in range(NEC):
        for tc in range(TCH):
            w = ([(S_V, v_wr)] + ns.xnT_done2) if (c == 0 and tc == 0) else ()
            last = (c == NEC - 1 and tc == TCH - 1)
            t_lg = p.op("tensor", lambda e, c=c, tc=tc, st=(c == 0), sp=(c == NEC - 1):
                        e.matmul(psm[0:8, tc * 512:(tc + 1) * 512],
                                 ns.t12[:, c * 8:(c + 1) * 8],
                                 big[:, c * T + tc * 512:c * T + tc * 512 + 512],
                                 start=st, stop=sp),
                        waits=w, inc=(S_T if last else None))
    s_lg = p.op("vector", lambda e: e.tensor_scalar_add(stA[0:8, 0:T], psm[0:8, 0:T],
                                                        ns.br_s[0:8, 0:1]),
                waits=[(S_T, t_lg)], inc=S_V)

    tr_evac = list(ns.ctx_free0)
    last_ttk = 0
    v_sT = 0

    def softmax_ttk(tb):
        nonlocal last_ttk
        sl = tb % 4
        ttk = p.op("tensor", lambda e, i=sl, r=tb * 128:
                   e.transpose(ns.ps_ctx[i][:, 0:8], stA[0:8, r:r + 128], ns.id_f[0:8, 0:8]),
                   waits=[(S_V, s_lg), tr_evac[sl]], inc=S_T)
        last_ttk = ttk
        vmx = p.op("vector", lambda e, i=sl:
                   e.reduce_max(sml[:, 55:56], ns.ps_ctx[i][:, 0:8], axis=AX.X, negate=True),
                   waits=[(S_T, ttk)], inc=S_V)
        sxp = p.op("scalar", lambda e, i=sl:
                   e.activation(ctok[:, 0:8], ns.ps_ctx[i][:, 0:8], AF.Exp,
                                bias=sml[:, 55:56], scale=1.0, accum_out=sml[:, 56:57]),
                   waits=[(S_V, vmx)], inc=S_S)
        vr1 = p.op("vector", lambda e: e.reciprocal(sml[:, 57:58], sml[:, 56:57]),
                   waits=[(S_S, sxp)], inc=S_V)
        vsc = p.op("vector", lambda e: e.tensor_scalar(
            ctok[:, 8:16], ctok[:, 0:8], sml[:, 57:58], 8.0, ALU.mult, ALU.mult),
            waits=[(S_V, vr1)], inc=S_V)
        return vsc

    def softmax_ttb(tb, vsc):
        nonlocal v_sT
        sl = tb % 4
        ttb = p.op("tensor", lambda e, i=sl:
                   e.transpose(ns.ps_ctx[i][0:8, 0:128], ctok[:, 8:16], ns.id_f[:]),
                   waits=[(S_V, vsc)], inc=S_T)
        v_sT = p.op("vector", lambda e, i=sl, r=tb * 128:
                    e.tensor_copy(stB[0:8, r:r + 128], ns.ps_ctx[i][0:8, 0:128]),
                    waits=[(S_T, ttb)] + ([ns.stB_free] if tb == 0 else []), inc=S_V)
        tr_evac[sl] = (S_V, v_sT)

    cpt = max(1, NEC // TBL)
    vsc_pend = None
    tb_next = 0
    for c in range(NEC):
        for tc in range(TCH):
            w = [acat_c, (S_V, s_lg)] if (c == 0 and tc == 0) else ()
            t_low = p.op("tensor", lambda e, c=c, tc=tc, st=(c == 0), sp=(c == NEC - 1):
                         e.matmul(psm[:, tc * 512:(tc + 1) * 512],
                                  acat_buf[:, c * 128:(c + 1) * 128],
                                  big[:, c * T + tc * 512:c * T + tc * 512 + 512],
                                  start=st, stop=sp),
                         waits=w, inc=(S_T if (c == NEC - 1 and tc == TCH - 1) else None))
        if c % cpt == cpt - 1 or c == NEC - 1:
            if vsc_pend is not None:
                softmax_ttb(*vsc_pend)
                vsc_pend = None
            if tb_next < TBL:
                vsc_pend = (tb_next, softmax_ttk(tb_next))
                tb_next += 1
    while tb_next < TBL or vsc_pend is not None:
        if vsc_pend is not None:
            softmax_ttb(*vsc_pend)
            vsc_pend = None
        if tb_next < TBL:
            vsc_pend = (tb_next, softmax_ttk(tb_next))
            tb_next += 1
    ns.wb_free[acat_sel] = (S_T, t_low)
    ns.ctx_free = [v for (_s, v) in tr_evac]

    v_scb = p.op("vector", lambda e: e.tensor_copy(ns.qk_raw[0:8, 0:T], stB[0:8, 0:T]),
                 waits=[(S_V, v_sT)], inc=S_V)
    ns.stB_free = (S_V, v_scb)
    lev = []
    for tc in range(TCH):
        w = [(S_T, t_low), (S_T, last_ttk)] if tc == 0 else ()
        le = p.op("vector", lambda e, tc=tc:
                  e.tensor_copy(stA[:, tc * 512:(tc + 1) * 512],
                                psm[:, tc * 512:(tc + 1) * 512]),
                  waits=w, inc=S_V)
        lev.append(le)

    for tc in range(TCH):
        w = [(S_V, v_scb), (S_V, lev[tc])]
        t_ex = p.op("tensor", lambda e, tc=tc:
                    e.matmul(psm[:, tc * 512:(tc + 1) * 512], ns.ee_b[:, 0:128],
                             ns.qk_raw[0:8, tc * 512:(tc + 1) * 512],
                             start=True, stop=True),
                    waits=w, inc=(S_T if tc == TCH - 1 else None))
    for tc in range(TCH):
        v_lo = p.op("vector", lambda e, tc=tc: e.tensor_tensor(
            ns.lows[:, tc * 512:(tc + 1) * 512], psm[:, tc * 512:(tc + 1) * 512],
            stA[:, tc * 512:(tc + 1) * 512], ALU.mult),
            waits=[(S_T, t_ex)] if tc == 0 else (), inc=(S_V if tc == TCH - 1 else None))
    ns.stA_free2 = (S_V, v_lo)
    ns.psm_free = (S_V, v_lo)


def emit_phase2(p, ns):
    T, E, NEC, TCH, NJ, HKV, GQ = ns.T, ns.E, ns.NEC, ns.TCH, ns.NJ, ns.HKV, ns.GQ
    S_T, S_V, S_S, S_G = ns.S_T, ns.S_V, ns.S_S, ns.S_G
    D_OUT = ns.D_OUT
    big, psm = ns.big, ns.ps_main
    qk, qb, kb, vT, t12, pT = ns.qk_raw, ns.q_b, ns.k_b, ns.vT_b, ns.t12, ns.pT
    cosb, sinb = ns.cos_b, ns.sin_b

    vtv = vT.rearrange("p (j c) -> p j c", c=129)[:, :, 128:129]
    p.op("gpsimd", lambda e, o=vtv: e.memset(o, 1.0),
         waits=list(ns.xnT_done2), inc=S_G)

    ns.proj_gate = {}
    qkraw_free = [0, 0]
    qb_free = [0, 0]
    kb_free = 0
    vt_free = 0
    ctok_free = [0, 0, 0, 0]
    ctst_free = 0
    pending = []
    pt_free = [0] * 6

    def proj(wb_, nout, conv_val, extra=(), tcs=None, gates=None):
        if tcs is None:
            tcs = list(range(TCH))
        lastv = 0
        for c in range(NEC):
            for tc in tcs:
                w = []
                if c == 0 and tc == tcs[0]:
                    w = [conv_val, ns.psm_free] + list(extra)
                gate = gates is not None and tc == tcs[-1] and c in gates
                lastv = p.op("tensor", lambda e, c=c, tc=tc, wb=wb_,
                             st=(c == 0), sp=(c == NEC - 1):
                             e.matmul(psm[:, tc * 512:(tc + 1) * 512],
                                      wb[:, c * 128:(c + 1) * 128],
                                      big[:, c * T + tc * 512:c * T + tc * 512 + 512],
                                      start=st, stop=sp),
                             waits=w,
                             inc=(S_T if (gate or (c == NEC - 1 and tc == tcs[-1]))
                                  else None))
                if gate:
                    ns.proj_gate[c] = lastv
        return lastv

    def evac_bf16(dst, tproj, extra=(), tcs=None):
        if tcs is None:
            tcs = list(range(TCH))
        lastv = 0
        for tc in tcs:
            w = [(S_T, tproj)] + list(extra) if tc == tcs[0] else ()
            lastv = p.op("scalar", lambda e, tc=tc, d=dst:
                         e.copy(d[:, tc * 512:(tc + 1) * 512], psm[:, tc * 512:(tc + 1) * 512]),
                         waits=w, inc=(S_S if tc == tcs[-1] else None))
        return lastv

    def rope(dst, war_free, c0=0, c1=None):
        if c1 is None:
            c1 = T
        p.op("vector", lambda e: e.tensor_tensor(
            t12[0:64, c0:c1], qk[64:128, c0:c1], sinb[64:128, c0:c1], ALU.mult))
        p.op("vector", lambda e: e.tensor_tensor(
            t12[64:128, c0:c1], qk[0:64, c0:c1], sinb[0:64, c0:c1], ALU.mult))
        p.op("vector", lambda e, d=dst: e.tensor_tensor(
            d[0:64, c0:c1], qk[0:64, c0:c1], cosb[0:64, c0:c1], ALU.mult),
            waits=[(S_T, war_free)] if war_free else ())
        v4 = p.op("vector", lambda e, d=dst: e.tensor_tensor(
            d[64:128, c0:c1], qk[64:128, c0:c1], cosb[64:128, c0:c1], ALU.mult),
            inc=S_V)
        p.op("vector", lambda e, d=dst: e.tensor_sub(
            d[0:64, c0:c1], d[0:64, c0:c1], t12[0:64, c0:c1]), waits=[(S_V, v4)])
        return p.op("vector", lambda e, d=dst: e.tensor_add(
            d[64:128, c0:c1], d[64:128, c0:c1], t12[64:128, c0:c1]), inc=S_V)

    HB = max(ns.TBL // 2, 1)

    def flush_pending(nmax=None):
        nonlocal ctst_free
        cnt = 0
        while pending and (nmax is None or cnt < nmax):
            tqb, tqB, ssc_val, first_half = pending.pop(0)
            ttc = p.op("tensor", lambda e, i=tqb, s=tqb:
                       e.transpose(ns.ps_ctx[i][:, 0:128], ns.ctok[:, s * 128:(s + 1) * 128],
                                   ns.id_f[:]),
                       waits=[(S_S, ssc_val)], inc=S_T)
            ctok_free[tqb] = ttc
            w = [(S_T, ttc)]
            if tqB % HB == 0 and ctst_free:
                w.append((D_OUT, ctst_free))
            vce = p.op("vector", lambda e, i=tqb, c0=(tqB % HB) * 128:
                       e.tensor_copy(ns.ctst[:, c0:c0 + 128], ns.ps_ctx[i][:, 0:128]),
                       waits=w, inc=S_V)
            ns.ctx_free[tqb] = vce
            if tqB == HB - 1 and ns.TBL > HB:
                ctst_free = p.op("scalar", lambda e, r=ns.cur_hg * 128, hw=HB * 128:
                                 e.dma_start(out=ns.ctxs[r:r + 128, 0:hw],
                                             in_=ns.ctst[:, 0:hw]),
                                 waits=[(S_V, vce)], inc=D_OUT, inck=16)
            cnt += 1

    wk_c = [None] * HKV
    wv_c = [None] * HKV
    wq_c = {}
    wk_buf = [None] * HKV
    wv_buf = [None] * HKV
    wq_buf = {}

    def load_slice(dram, col0, tag):
        sel = ns.wsel
        ns.wsel ^= 1
        cval = load_weight_cols(p, ns, dram[:, col0:col0 + 128], ns.wb[sel], E,
                                ns.wb_free[sel])
        return cval, sel

    def load_slice_part(dram, col0, part, sel):
        r0 = part * (E // 2)
        return load_weight_cols(p, ns, dram[r0:r0 + E // 2, col0:col0 + 128],
                                ns.wb[sel], E // 2, ns.wb_free[sel],
                                c_base=part * (E // 256))

    wk_c[0], s0 = load_slice(ns.wkT, 0, "wk0")
    wk_buf[0] = s0
    wv_c[0], s1 = load_slice(ns.wvT, 0, "wv0")
    wv_buf[0] = s1

    for h in range(HKV):
        tkp = proj(ns.wb[wk_buf[h]], T, wk_c[h])
        ns.wb_free[wk_buf[h]] = (S_T, tkp)
        skv = evac_bf16(qk, tkp, extra=([(S_V, max(qkraw_free))]
                                        if max(qkraw_free) else []))
        ns.psm_free = (S_S, skv)
        wq_c[(h, 0)], wq_buf[(h, 0)] = load_slice(ns.wqT, (h * GQ) * 128, "wq")
        p.waits("vector", [(S_S, skv)] + ([(S_T, kb_free)] if kb_free else []))
        krope = rope(kb, 0)
        qkraw_free = [krope, krope]

        tvp = proj(ns.wb[wv_buf[h]], T, wv_c[h])
        ns.wb_free[wv_buf[h]] = (S_T, tvp)
        svv = 0
        for tc in range(TCH):
            w = [(S_T, tvp)] + ([ns.stA_free2] if tc == 0 else [])
            svv = p.op("scalar", lambda e, tc=tc:
                       e.activation(ns.stA[:, tc * 512:(tc + 1) * 512],
                                    psm[:, tc * 512:(tc + 1) * 512], AF.Copy,
                                    bias=0.0, scale=1.0),
                       waits=w if tc == 0 else (), inc=(S_S if tc == TCH - 1 else None))
        ns.psm_free = (S_S, svv)
        flush_pending()
        last_vtr = 0
        for j in range(NJ):
            sl = j % 4
            w = [(S_V, ns.ctx_free[sl])]
            if j == 0:
                w.append((S_S, svv))
            ttv = p.op("tensor", lambda e, j=j, i=sl:
                       e.transpose(ns.ps_ctx[i][:, 0:128], ns.stA[:, j * 128:(j + 1) * 128],
                                   ns.id_f[:]),
                       waits=w, inc=S_T)
            if j == NJ - 1:
                ns.stA_free2 = (S_T, ttv)
            w2 = [(S_T, ttv)]
            if j == 0 and vt_free:
                w2.append((S_T, vt_free))
            vev = p.op("vector", lambda e, j=j, i=sl:
                       e.tensor_copy(vT[:, j * 129:j * 129 + 128], ns.ps_ctx[i][:, 0:128]),
                       waits=w2, inc=S_V)
            ns.ctx_free[sl] = vev
            last_vtr = vev

        for g in range(GQ):
            hg = h * GQ + g
            ns.cur_hg = hg
            if h == HKV - 1 and g == GQ - 1:
                ns.strip_prev = ctst_free
            hsplit = (TCH + 1) // 2
            h_tcs = [list(range(0, hsplit)), list(range(hsplit, TCH))]
            sqes, vqrs = [], []
            tqp = 0
            final_g = (h == HKV - 1 and g == GQ - 1)
            lasthh = 1 if h_tcs[1] else 0
            NSg = HKV * GQ
            bndg = [round(i * (NSg - 1) / 4) for i in range(5)]
            ns.gate_chunks = [min(bndg[k + 1] - 1, NEC - 1) for k in range(4)]
            for hh, tcs in enumerate(h_tcs):
                if not tcs:
                    sqes.append(sqes[-1])
                    vqrs.append(vqrs[-1])
                    continue
                c0, c1 = tcs[0] * 512, (tcs[-1] + 1) * 512
                gts = set(ns.gate_chunks) if (final_g and hh == lasthh) else None
                tqp = proj(ns.wb[wq_buf[(h, g)]], T, wq_c[(h, g)], tcs=tcs,
                           gates=gts)
                wex = [(S_V, qkraw_free[hh])] if qkraw_free[hh] else []
                sq = evac_bf16(qk, tqp, extra=wex, tcs=tcs)
                p.waits("vector", [(S_S, sq)])
                vq = rope(qb, qb_free[hh], c0, c1)
                sqes.append(sq)
                vqrs.append(vq)
                qkraw_free[hh] = vq
            ns.wb_free[wq_buf[(h, g)]] = (S_T, tqp)
            if h == HKV - 1 and g == GQ - 1:
                ns.last_proj = tqp
            half_free = [sqes[0], sqes[1]]

            allg = []
            for tqc in range(TCH):
                njj = 4 * tqc + 4
                gs = [list(range(j0, min(j0 + 2, njj))) for j0 in range(0, njj, 2)]
                for gi, js in enumerate(gs):
                    allg.append((tqc, js, gi == 0, gi == len(gs) - 1))
            stop_val = [0, 0, 0, 0]
            mvals_all = {}
            sxp_all = {}
            vqr_waited = [False, False]
            LAG = 2
            for n in range(len(allg) + LAG):
                trig0 = min(1, len(allg) - 1)
                trig1 = min(6, len(allg) - 1)
                parts_now = []
                if n == trig0:
                    parts_now.append(0)
                    if trig1 == trig0:
                        parts_now.append(1)
                elif n == trig1:
                    parts_now.append(1)
                for part in parts_now:
                    if g + 1 < GQ:
                        if part == 0:
                            wq_buf[(h, g + 1)] = ns.wsel
                            ns.wsel ^= 1
                        wq_c[(h, g + 1)] = load_slice_part(
                            ns.wqT, (h * GQ + g + 1) * 128, part,
                            wq_buf[(h, g + 1)])
                    elif h + 1 < HKV:
                        if part == 0:
                            wk_buf[h + 1] = ns.wsel
                            ns.wsel ^= 1
                        wk_c[h + 1] = load_slice_part(
                            ns.wkT, (h + 1) * 128, part, wk_buf[h + 1])
                if n == min(12, len(allg)) and g == GQ - 1 and h + 1 < HKV:
                    wv_c[h + 1], wv_buf[h + 1] = \
                        load_slice(ns.wvT, (h + 1) * 128, "wv")
                if n < len(allg):
                    tqc, js, _, is_last = allg[n]
                    hf = n % 2
                    need = 0 if tqc < hsplit else 1
                    last_sc = 0
                    for idx, j in enumerate(js):
                        w = []
                        if idx == 0:
                            w = [(S_S, half_free[hf])]
                            if not vqr_waited[need]:
                                w.append((S_V, vqrs[need]))
                                vqr_waited[need] = True
                        last_sc = p.op("tensor", lambda e, j=j, hf=hf, idx=idx, tqc=tqc:
                                       e.matmul(psm[:, hf * 1024 + idx * 512:
                                                    hf * 1024 + (idx + 1) * 512],
                                                kb[:, j * 128:(j + 1) * 128],
                                                qb[:, tqc * 512:(tqc + 1) * 512],
                                                start=True, stop=True),
                                       waits=w,
                                       inc=(S_T if idx == len(js) - 1 else None))
                    if tqc == hsplit - 1 and is_last:
                        qb_free[0] = last_sc
                    if n == len(allg) - 1:
                        qb_free[1] = last_sc
                        if qb_free[0] == 0:
                            qb_free[0] = last_sc
                        if g == GQ - 1:
                            kb_free = last_sc
                    wex = [(S_T, last_sc)]
                    for idx in range(len(js)):
                        if pt_free[(n % 3) * 2 + idx]:
                            wex.append((S_T, pt_free[(n % 3) * 2 + idx]))
                    nsec = len(js)
                    trim = min((j % 4) * 128 if j // 4 == tqc else 0 for j in js)
                    sec = n % 3
                    sxp = p.op("scalar", lambda e, hf=hf, sc=sec, nn=nsec, t0=trim:
                               e.activation(pT[:, sc * 1024 + t0:sc * 1024 + nn * 512],
                                            psm[:, hf * 1024 + t0:hf * 1024 + nn * 512],
                                            AF.Exp, bias=ns.sml[:, 54:55], scale=ISQ),
                               waits=wex, inc=S_S)
                    half_free[hf] = sxp
                    sxp_all[n] = sxp
                    for idx, j in enumerate(js):
                        if j // 4 == tqc:
                            o = (n % 3) * 1024 + idx * 512 + (j % 4) * 128
                            mvals_all[(n, j)] = p.op(
                                "gpsimd", lambda e, o=o:
                                e.tensor_tensor(pT[:, o:o + 128], pT[:, o:o + 128],
                                                ns.cm_b[:], ALU.mult),
                                waits=[(S_S, sxp)], inc=S_G)
                if n >= LAG:
                    m = n - LAG
                    tqc0, js0, first0, last0 = allg[m]
                    hf0 = m % 3
                    if first0:
                        flush_pending()
                    pv_first = True
                    for idx, j in enumerate(js0):
                        tqb_lo = max(0, j - 4 * tqc0)
                        for tqb in range(tqb_lo, 4):
                            tqB = 4 * tqc0 + tqb
                            w = []
                            if tqb == tqb_lo and ((m, j) in mvals_all or pv_first):
                                w.append((S_G, mvals_all[(m, j)])
                                         if (m, j) in mvals_all
                                         else (S_S, sxp_all[m]))
                            if j == 0 and pv_first:
                                w.append((S_V, max(ns.ctx_free)))
                            pv_first = False
                            do_inc = (j == tqB) or (tqb == 3)
                            val = p.op("tensor",
                                       lambda e, j=j, hf=hf0, idx=idx, tqb=tqb, tqB=tqB:
                                       e.matmul(ns.ps_ctx[tqb][:, 0:129],
                                                pT[:, hf * 1024 + idx * 512 + tqb * 128:
                                                   hf * 1024 + idx * 512 + tqb * 128 + 128],
                                                vT[:, j * 129:(j + 1) * 129],
                                                start=(j == 0), stop=(j == tqB)),
                                       waits=w, inc=(S_T if do_inc else None))
                            if j == tqB:
                                stop_val[tqb] = val
                            if tqb == 3:
                                pt_free[hf0 * 2 + idx] = val
                            if g == GQ - 1 and j == NJ - 1:
                                vt_free = val
                    if last0:
                        for tqb in range(4):
                            tqB = 4 * tqc0 + tqb
                            vrc = p.op("vector", lambda e, i=tqb, c=tqB:
                                       e.reciprocal(ns.rsb[:, c:c + 1],
                                                    ns.ps_ctx[i][:, 128:129]),
                                       waits=[(S_T, stop_val[tqb])], inc=S_V)
                            w2 = [(S_V, vrc)]
                            if ctok_free[tqb]:
                                w2.append((S_T, ctok_free[tqb]))
                            ssc = p.op("scalar", lambda e, i=tqb, s=tqb, c=tqB:
                                       e.activation(ns.ctok[:, s * 128:(s + 1) * 128],
                                                    ns.ps_ctx[i][:, 0:128], AF.Identity,
                                                    bias=0.0, scale=ns.rsb[:, c:c + 1]),
                                       waits=w2, inc=S_S)
                            pending.append((tqb, tqB, ssc, None))
            flush_pending()
            h0 = HB * 128 if ns.TBL > HB else 0
            dst = p.op("scalar", lambda e, r=hg * 128, c0=h0:
                       e.dma_start(out=ns.ctxs[r:r + 128, c0:T],
                                   in_=ns.ctst[:, 0:T - h0]),
                       waits=[(S_V, max(ns.ctx_free))], inc=D_OUT, inck=16)
            ctst_free = dst
            ns.psm_free = (S_S, max(half_free))
    NS = HKV * GQ
    bnd = [round(i * (NS - 1) / 4) for i in range(5)]
    ns.big_bnd = bnd
    ns.big_piece = []
    psem = [ns.D_B2[0], ns.D_B2[1], ns.D_B2[2], ns.D_B2[3]]
    for k in range(4):
        a, b = bnd[k], bnd[k + 1]
        if b <= a:
            ns.big_piece.append(None)
            continue
        csrc = ns.ctxs[a * 128:b * 128, :].rearrange("(c p) t -> p c t", p=128)
        cdst = big.rearrange("p (c t) -> p c t", t=T)[:, a:b, :]
        w = [(ns.D_OUT, ns.strip_prev, True),
             (S_T, ns.proj_gate[ns.gate_chunks[k]])]
        v = p.op("sync", lambda e, o=cdst, i=csrc: e.dma_start(out=o, in_=i),
                 waits=w, inc=psem[k], inck=16)
        ns.big_piece.append((psem[k], v))
    ns.wo_pre = {}
    for eb in range(2):
        sel = ns.wsel
        ns.wsel ^= 1
        cval = load_weight_cols(p, ns, ns.woT[:, eb * 128:(eb + 1) * 128],
                                ns.wb[sel], E, ns.wb_free[sel])
        ns.wo_pre[eb] = (cval, sel)
    ns.ctst_dma_final = ctst_free


def emit_phase3(p, ns):
    T, E, NEC, TCH = ns.T, ns.E, ns.NEC, ns.TCH
    S_T, S_V, S_S, S_G = ns.S_T, ns.S_V, ns.S_S, ns.S_G
    D_OUT, D_BIG, D_C = ns.D_OUT, ns.D_BIG, ns.D_C
    big, psB, psW = ns.big, ns.ps_B, ns.ps_W

    NS = E // 128
    p.waits("sync", [(D_OUT, ns.ctst_dma_final)])
    d_ctx = p.op("sync", lambda e, o=big[:, (NS - 1) * T:NS * T],
                 i=ns.ctxs[(NS - 1) * 128:NS * 128, :]:
                 e.dma_start(out=o, in_=i), inc=D_BIG, inck=16)

    def big_gate(c):
        if c >= NS - 1:
            return (D_BIG, d_ctx)
        for k in range(4):
            if c < ns.big_bnd[k + 1]:
                return ns.big_piece[k] or (D_BIG, d_ctx)
        return (D_BIG, d_ctx)
    gate_cs = sorted(set([0] + ns.big_bnd[1:4] + [NS - 1]))

    v_bc = 0
    for h0 in range(0, E, 2048):
        wdt = min(2048, E - h0)
        d = p.op("sync", lambda e, h0=h0, w=wdt:
                 e.dma_start(out=ns.stA[:, 0:w], in_=ns.bcat[:, h0:h0 + w]),
                 waits=[(S_V, v_bc)] if v_bc else [ns.stA_free2], inc=D_C, inck=16)
        v_bc = p.op("vector", lambda e, h0=h0, w=wdt:
                    e.tensor_copy(ns.bc_b[:, h0:h0 + w], ns.stA[:, 0:w]),
                    waits=[(D_C, d)], inc=S_V)
        ns.stA_free3 = (S_V, v_bc)

    wo_c = {eb: cv for eb, (cv, _s) in ns.wo_pre.items()}
    wo_buf = {eb: s for eb, (_cv, s) in ns.wo_pre.items()}

    def load_wo(eb):
        sel = ns.wsel
        ns.wsel ^= 1
        wo_c[eb] = load_weight_cols(p, ns, ns.woT[:, eb * 128:(eb + 1) * 128],
                                    ns.wb[sel], E, ns.wb_free[sel], alt=True)
        wo_buf[eb] = sel
    xT_free = [0, 0]
    psW_free = 0
    s_g_prev = 0
    d_out = 0
    ps2 = [psW, psB]
    for eb in range(NEC):
        wo_ps = ps2[eb % 2]
        b_ps = ps2[1 - eb % 2]
        xsel = eb % 2
        D_X = ns.D_XA if xsel == 0 else ns.D_XB
        d_x = p.op("sync", lambda e, r=eb * 128, s=xsel:
                   e.dma_start(out=ns.xTb[:, s * T:s * T + T], in_=ns.xT[r:r + 128, :]),
                   waits=[(S_G, xT_free[xsel])] if xT_free[xsel] else (),
                   inc=D_X, inck=16)
        t_B = 0
        for tc in range(TCH):
            w = []
            if tc == 0 and eb == 0:
                w.append((S_V, v_bc))
            if psW_free:
                w.append((S_V, psW_free[tc]))
            t_B = p.op("tensor", lambda e, eb=eb, tc=tc, ps=b_ps:
                       e.matmul(ps[:, tc * 512:(tc + 1) * 512],
                                ns.bc_b[:, eb * 128:(eb + 1) * 128],
                                ns.lows[:, tc * 512:(tc + 1) * 512],
                                start=True, stop=True),
                       waits=w, inc=(S_T if tc == TCH - 1 else None))
        s_g = 0
        for tc in range(TCH):
            w = [(S_T, t_B)]
            if tc == 0 and d_out:
                w.append((D_OUT, d_out))
            s_g = p.op("scalar", lambda e, tc=tc, ps=b_ps:
                       e.activation(ns.outb[:, tc * 512:(tc + 1) * 512],
                                    ps[:, tc * 512:(tc + 1) * 512], AF.Silu,
                                    bias=0.0, scale=2.0),
                       waits=w if tc == 0 else (), inc=(S_S if tc == TCH - 1 else None))
        t_W = 0
        wo_strip = []
        for tc in range(TCH):
            for c in range(NEC):
                w = []
                if c == 0 and tc == 0:
                    w = [wo_c[eb]]
                    if s_g_prev:
                        w.append((S_S, s_g_prev))
                if eb == 0 and tc == 0 and c in gate_cs:
                    w.append(big_gate(c))
                t_W = p.op("tensor", lambda e, c=c, tc=tc, wb=ns.wb[wo_buf[eb]],
                           ps=wo_ps, st=(c == 0), sp=(c == NEC - 1):
                           e.matmul(ps[:, tc * 512:(tc + 1) * 512],
                                    wb[:, c * 128:(c + 1) * 128],
                                    big[:, c * T + tc * 512:c * T + tc * 512 + 512],
                                    start=st, stop=sp),
                           waits=w,
                           inc=(S_T if c == NEC - 1 else None))
            wo_strip.append(t_W)
        ns.wb_free[wo_buf[eb]] = (S_T, t_W)
        v_strips = []
        v_cb = 0
        for tc in range(TCH):
            w = [(S_T, wo_strip[tc])]
            if tc == 0:
                w.append((S_S, s_g))
            v_cb = p.op("vector", lambda e, tc=tc, ps=wo_ps:
                        e.tensor_tensor(ns.outb[:, tc * 512:(tc + 1) * 512],
                                        ps[:, tc * 512:(tc + 1) * 512],
                                        ns.outb[:, tc * 512:(tc + 1) * 512], ALU.mult),
                        waits=w, inc=S_V)
            v_strips.append(v_cb)
        psW_free = v_strips
        s_g_prev = s_g
        if eb + 2 < NEC:
            load_wo(eb + 2)
        g_cb = 0
        for tc in range(TCH):
            w = [(S_V, v_cb), (D_X, d_x)] if tc == 0 else ()
            g_cb = p.op("gpsimd", lambda e, tc=tc, s=xsel:
                        e.tensor_add(ns.outb[:, tc * 512:(tc + 1) * 512],
                                     ns.outb[:, tc * 512:(tc + 1) * 512],
                                     ns.xTb[:, s * T + tc * 512:s * T + (tc + 1) * 512]),
                        waits=w, inc=(S_G if tc == TCH - 1 else None))
        xT_free[xsel] = g_cb
        d_out = p.op("scalar", lambda e, r=eb * 128:
                     e.dma_start(out=ns.outT[r:r + 128, :], in_=ns.outb[:, 0:T]),
                     waits=[(S_G, g_cb)], inc=D_OUT, inck=16)
    p.waits("scalar", [(D_OUT, d_out)])



def _rope_tables(T):
    DH, theta, factor, hi_f, lo_f, orig = 128, 500000.0, 32.0, 4.0, 1.0, 8192
    i = np.arange(0, DH, 2, dtype=np.float64)
    inv = 1.0 / theta ** (i / DH)
    wav = 2.0 * math.pi / inv
    cut_lo, cut_hi = orig / hi_f, orig / lo_f
    inv = np.where(wav > cut_hi, inv / factor, inv)
    smooth = (orig / wav - lo_f) / (hi_f - lo_f)
    smoothed = (1.0 - smooth) * inv / factor + smooth * inv
    is_med = ~(wav < cut_hi) & ~(wav > cut_lo)
    inv = np.where(is_med, smoothed, inv)
    pos = np.arange(T, dtype=np.float64)
    fr = pos[:, None] * inv[None, :]
    emb = np.concatenate([fr, fr], axis=1).T
    out = []
    for tab in (np.cos(emb), np.sin(emb)):
        hi = tab.astype(np.float32)
        lo = (tab - hi.astype(np.float64)).astype(np.float32)
        out += [np.ascontiguousarray(hi), np.ascontiguousarray(lo)]
    return out


def make_inputs(cfg, x, Wq, Wk, Wv, Wo, A, B, Wr, br):
    T, E = cfg["T"], cfg["E"]
    f = np.float32
    ch, cl, sh, sl = _rope_tables(T)
    shared = dict(
        wqT=np.ascontiguousarray(np.asarray(Wq, f).T),
        wkT=np.ascontiguousarray(np.asarray(Wk, f).T),
        wvT=np.ascontiguousarray(np.asarray(Wv, f).T),
        woT=np.ascontiguousarray(np.asarray(Wo, f).T),
        acatT=np.ascontiguousarray(np.asarray(A, f).reshape(128, E).T),
        bcat=np.ascontiguousarray(np.asarray(B, f).transpose(0, 2, 1).reshape(128, E)),
        wrT=np.ascontiguousarray(np.asarray(Wr, f).T),
        br=np.ascontiguousarray(np.asarray(br, f).reshape(8, 1)),
        cos_hi=ch, cos_lo=cl, sin_hi=sh, sin_lo=sl,
        identf=np.eye(128, dtype=f),
        cmaskf=np.ascontiguousarray(
            (np.arange(128)[:, None] <= np.arange(128)[None, :]).astype(f)),
        eexpf=np.ascontiguousarray(
            (np.arange(128)[None, :] // 16 == np.arange(8)[:, None]).astype(f)),
    )
    in_maps = []
    for b in range(x.shape[0]):
        m = dict(shared)
        m["x_tok"] = np.ascontiguousarray(np.asarray(x[b], f))
        m["xT"] = np.ascontiguousarray(np.asarray(x[b], f).T)
        in_maps.append(m)
    return in_maps


def kernel(L0_attn_x, L0_attn_Wq, L0_attn_Wk, L0_attn_Wv, L0_attn_Wo,
           L0_ffn_A, L0_ffn_B, L0_ffn_Wr, L0_ffn_br):
    cfg = FULL
    x = np.asarray(L0_attn_x, np.float32)
    in_maps = make_inputs(cfg, x, L0_attn_Wq, L0_attn_Wk, L0_attn_Wv, L0_attn_Wo,
                          L0_ffn_A, L0_ffn_B, L0_ffn_Wr, L0_ffn_br)
    nc = build_nc(cfg)
    res = run_bass_kernel_spmd(nc, in_maps, core_ids=list(range(len(in_maps))))
    out = np.stack([np.asarray(r["outT"], np.float32).T for r in res.results])
    return np.ascontiguousarray(out)
